# version 10
# baseline (speedup 1.0000x reference)
import numpy as np
import ml_dtypes

import concourse.bass as bass
import concourse.mybir as mybir
from concourse.bass_utils import run_bass_kernel_spmd

F32 = mybir.dt.float32
BF16 = mybir.dt.bfloat16
I32 = mybir.dt.int32
AF = mybir.ActivationFunctionType
ALU = mybir.AluOpType
AX = mybir.AxisListType

T = 2048
D = 2048
H = 8192
R = 16
NH = 16
HD = 128
SEQ = 4096
DC = D // 128
HC = H // 128
TT = 512
NTIL = T // TT

_STAGE = "full"

INV_D = 1.0 / D
EPS = 1e-5
EXP_BIAS = -20.0
INV_SQRT_HD = float(1.0 / np.sqrt(128.0))
HORDER = [0, 8, 1, 9, 2, 10, 3, 11, 4, 12, 5, 13, 6, 14, 7, 15]
PAIRS = [[0, 1], [2, 3], [4, 5], [6, 7]]


class Emitter:
    def wge(self, eng, sem, v):
        if v <= 0:
            return
        eng.wait_ge(sem, v)


em = Emitter()


def sidx(h_idx, qc, kt):
    return (h_idx * 4 + qc) * 32 + kt


def build(stage):
    nc = bass.Bass("TRN2", target_bir_lowering=False)

    ap = lambda t: t.ap()
    x_in = ap(nc.dram_tensor("x_in", [T, D], F32, kind="ExternalInput"))
    wq_in = ap(nc.dram_tensor("wq_in", [D, D], F32, kind="ExternalInput"))
    wk_in = ap(nc.dram_tensor("wk_in", [D, D], F32, kind="ExternalInput"))
    wv_in = ap(nc.dram_tensor("wv_in", [D, D], F32, kind="ExternalInput"))
    wo_in = ap(nc.dram_tensor("wo_in", [D, D], F32, kind="ExternalInput"))
    gate_in = ap(nc.dram_tensor("gate_in", [1, NH], F32, kind="ExternalInput"))
    wupq_in = ap(nc.dram_tensor("wupq_in", [H, D], I32, kind="ExternalInput"))
    wups_in = ap(nc.dram_tensor("wups_in", [H, 32], F32, kind="ExternalInput"))
    bup_in = ap(nc.dram_tensor("bup_in", [128, HC], F32, kind="ExternalInput"))
    ua_in = ap(nc.dram_tensor("ua_in", [128, DC * R], F32, kind="ExternalInput"))
    ub_in = ap(nc.dram_tensor("ub_in", [R, H], F32, kind="ExternalInput"))
    wdnq_in = ap(nc.dram_tensor("wdnq_in", [D, H], I32, kind="ExternalInput"))
    wdns_in = ap(nc.dram_tensor("wdns_in", [D, 128], F32, kind="ExternalInput"))
    bdn_in = ap(nc.dram_tensor("bdn_in", [1, D], F32, kind="ExternalInput"))
    da_in = ap(nc.dram_tensor("da_in", [128, HC * R], F32, kind="ExternalInput"))
    db_in = ap(nc.dram_tensor("db_in", [R, D], F32, kind="ExternalInput"))
    cos_in = ap(nc.dram_tensor("cos_in", [1024, T], F32, kind="ExternalInput"))
    sin_in = ap(nc.dram_tensor("sin_in", [1024, T], F32, kind="ExternalInput"))
    ident_in = ap(nc.dram_tensor("ident_in", [128, 128], BF16, kind="ExternalInput"))

    out_d = ap(nc.dram_tensor("out_d", [T, D], F32, kind="ExternalOutput"))

    resid1 = ap(nc.dram_tensor("resid1", [T, D], F32))
    resid2 = ap(nc.dram_tensor("resid2", [T, D], F32))
    wupT_c = ap(nc.dram_tensor("wupT_c", [HC * 128, D], BF16))
    wdnT_c = ap(nc.dram_tensor("wdnT_c", [HC * 128, D], BF16))
    wq_bf = ap(nc.dram_tensor("wq_bf", [D, D], BF16))
    wk_bf = ap(nc.dram_tensor("wk_bf", [D, D], BF16))
    wv_bf = ap(nc.dram_tensor("wv_bf", [D, D], BF16))
    wo_bf = ap(nc.dram_tensor("wo_bf", [D, D], BF16))
    wqT_c = ap(nc.dram_tensor("wqT_c", [DC * 128, D], BF16))
    wkT_c = ap(nc.dram_tensor("wkT_c", [DC * 128, D], BF16))
    wvT_c = ap(nc.dram_tensor("wvT_c", [DC * 128, D], BF16))
    woT_c = ap(nc.dram_tensor("woT_c", [DC * 128, D], BF16))
    cmb_d = ap(nc.dram_tensor("cmb_d", [D, T], BF16))
    sh_k = ap(nc.dram_tensor("sh_k", [2 * NH * 128, T], BF16, addr_space="Shared"))
    sh_v = ap(nc.dram_tensor("sh_v", [2 * NH * T, 128], BF16, addr_space="Shared"))
    kstg_k = ap(nc.dram_tensor("kstg_k", [NH * 128, T], BF16))
    kstg_v = ap(nc.dram_tensor("kstg_v", [NH * T, 128], BF16))
    bar_i = ap(nc.dram_tensor("bar_i", [1, 4], F32))
    bar_o = ap(nc.dram_tensor("bar_o", [1, 4], F32))

    from contextlib import ExitStack
    with ExitStack() as ctx:
        e = ctx.enter_context
        offs = {}

        def sbuf(name, shape, dtype):
            t = e(nc.sbuf_tensor(name, shape, dtype))
            sz = shape[1] * ({F32: 4, BF16: 2, I32: 4}[dtype])
            offs[name] = nc.sbuf_base - sz
            return t

        def overlay(name, shape, dtype, at, plus=0):
            return nc.alloc_sbuf_tensor_at(name, shape, dtype, offset=offs[at] + plus)

        ident = sbuf("ident", [128, 128], BF16)
        bupc = sbuf("bupc", [128, HC], F32)
        bdnr = sbuf("bdnr", [1, D], BF16)
        ones1 = sbuf("ones1", [1, 128], BF16)
        loraA = sbuf("loraA", [128, DC * R], BF16)
        loraB = sbuf("loraB", [R, H], BF16)
        dnA = sbuf("dnA", [128, HC * R], BF16)
        dnB = sbuf("dnB", [R, D], BF16)
        grow = sbuf("grow", [1, NH], F32)
        growb = sbuf("growb", [1, NH], BF16)
        gcol = sbuf("gcol", [128, NH], F32)
        gicol = sbuf("gicol", [128, NH], F32)
        cstc = sbuf("cstc", [128, 4], F32)
        zrow = sbuf("zrow", [1, 4], F32)
        zb512 = sbuf("zb512", [1, 512], BF16)
        memSB = sbuf("memSB", [128, 130], BF16)
        den_sb = sbuf("den_sb", [128, 8], F32)

        xt = [sbuf(f"xt{i}", [128, D], F32) for i in range(4)]
        sq = sbuf("sq", [128, D], F32)
        lntm = [sbuf(f"lntm{i}", [128, D], BF16) for i in range(2)]
        stat = sbuf("stat", [128, 16], F32)
        ln1fm = sbuf("ln1fm", [128, DC * TT], BF16)
        x2fm = sbuf("x2fm", [128, HC * TT], BF16)
        outst = [sbuf(f"outst{i}", [128, 512], F32) for i in range(2)]
        xa_sb = sbuf("xa_sb", [R, TT], BF16)
        xda_sb = sbuf("xda_sb", [R, TT], BF16)
        wbufA = [sbuf(f"wbufA{i}", [128, D], BF16) for i in range(2)]
        wbufB = [sbuf(f"wbufB{i}", [128, 512], BF16) for i in range(4)]
        bufQ = [sbuf(f"bufQ{i}", [128, D], F32) for i in range(2)]
        bufD = [sbuf(f"bufD{i}", [128, D], BF16) for i in range(2)]
        bufT = [sbuf(f"bufT{i}", [128, D], BF16) for i in range(2)]
        bufS = [sbuf(f"bufS{i}", [128, 32], F32) for i in range(2)]

        qrp4 = [[overlay("qrp00", [128, T], BF16, "lntm0"),
                 overlay("qrp01", [128, T], BF16, "lntm1")],
                [overlay("qrp10", [128, T], BF16, "xt1"),
                 overlay("qrp11", [128, T], BF16, "xt1", plus=4096)]]

        klo_sb = overlay("klo_sb", [128, 512], F32, "sq")
        tmpAB = overlay("tmpAB", [128, 1024], F32, "sq", plus=2048)
        tmpP = overlay("tmpP", [128, 128], BF16, "sq", plus=6144)
        k_buf = [overlay("k_buf0", [128, SEQ], BF16, "bufQ0"),
                 overlay("k_buf1", [128, SEQ], BF16, "bufQ1")]
        awbuf = [overlay("awbuf0", [128, D], BF16, "bufQ0"),
                 overlay("awbuf1", [128, D], BF16, "bufQ1")]
        v_buf = [overlay("v_buf0", [128, 32 * 130], BF16, "bufD0"),
                 overlay("v_buf1", [128, 32 * 130], BF16, "bufD0", plus=8320)]
        kf_tm = overlay("kf_tm", [128, SEQ], BF16, "ln1fm")
        ptb = [overlay("ptb0", [128, 1024], BF16, "ln1fm", plus=8192),
               overlay("ptb1", [128, 1024], BF16, "ln1fm", plus=10240)]
        q_feat = overlay("q_feat", [128, T], BF16, "ln1fm", plus=12288)
        memout = overlay("memout", [128, 16 * 130], BF16, "outst0")
        kst = [overlay("kst0", [128, 512], BF16, "wbufB0", plus=2048),
               overlay("kst1", [128, 512], BF16, "wbufB0", plus=3072)]
        cmbTM = [overlay("cmbTM0", [128, 512], BF16, "wbufB0"),
                 overlay("cmbTM1", [128, 512], BF16, "wbufB0", plus=1024)]
        cs2c = [overlay("cs2c0", [128, 512], F32, "xt0"),
                overlay("cs2c1", [128, 512], F32, "xt0", plus=2048)]
        cs2s = [overlay("cs2s0", [128, 512], F32, "xt0", plus=4096),
                overlay("cs2s1", [128, 512], F32, "xt0", plus=6144)]
        tmpP16 = overlay("tmpP16", [128, 2048], BF16, "xt2")
        cmbOut = [overlay("cmbOut0", [128, 1024], BF16, "xt3"),
                  overlay("cmbOut1", [128, 1024], BF16, "xt3", plus=2048)]
        wvbuf = overlay("wvbuf", [128, 16 * 512], BF16, "ln1fm")
        vst = [overlay("vst0", [128, 512], BF16, "wbufA0"),
               overlay("vst1", [128, 512], BF16, "wbufA1")]
        woCbuf = overlay("woCbuf", [128, 16 * 512], BF16, "ln1fm")

        PS = [e(nc.psum_tensor(f"PS{i}", [128, 1024], F32)) for i in range(4)]
        psA = [PS[0][:, 0:512], PS[0][:, 512:1024]]
        psB = [PS[1][:, 0:512], PS[1][:, 512:1024], PS[2][:, 0:512], PS[2][:, 512:1024]]
        psXA = PS[1][0:R, 0:512]
        psDQ = PS[3].ap().bitcast(BF16)[:, 0:1024]
        psDQ3 = psDQ.rearrange("p (a c) -> p a c", c=128)
        psLN = PS[3].ap().bitcast(BF16)[:, 1024:2048]
        psLN3 = psLN.rearrange("p (a c) -> p a c", c=128)
        psPJ = PS[3][:, 0:512]
        psMEM = PS[3][:, 0:130]
        psS = [PS[0][:, 0:512], PS[0][:, 512:1024]]
        psPV = [PS[1][:, 0:130], PS[1][:, 512:642], PS[2][:, 0:130], PS[2][:, 512:642]]
        psMO = [PS[1][:, 0:130], PS[1][:, 130:260], PS[1][:, 260:390],
                PS[1][:, 512:642], PS[1][:, 642:772], PS[1][:, 772:902],
                PS[2][:, 0:130], PS[2][:, 130:260]]
        psV = [PS[0][:, 0:512], PS[0][:, 512:1024]]
        psG = PS[3][:, 512:528]

        def sem(name):
            return e(nc.semaphore(name))

        sXt = [sem(f"sXt{i}") for i in range(4)]; sC = sem("sC")
        dqL = [sem("dqL0"), sem("dqL1")]
        dqV = sem("dqV"); dqT = sem("dqT"); dqE = sem("dqE"); dqW = sem("dqW")
        wAL = [sem("wAL0"), sem("wAL1")]; wAF = sem("wAF")
        wBL = [sem(f"wBL{i}") for i in range(4)]; wBF = sem("wBF")
        lnQ = sem("lnQ"); lnR = sem("lnR"); lnV = sem("lnV"); lnST = sem("lnST")
        lnN = sem("lnN"); lnT = sem("lnT"); lnE = sem("lnE")
        mmXA = sem("mmXA"); eXA = sem("eXA"); mmXDA = sem("mmXDA"); eXDA = sem("eXDA")
        mmA = sem("mmA"); evA = sem("evA"); evB = sem("evB")
        oS = [sem("oS0"), sem("oS1")]
        vsy = sem("vsy"); ssy = sem("ssy"); ini = sem("ini")
        awc = sem("awc"); awt = [sem("awt0"), sem("awt1")]; aww = sem("aww")
        mmP = sem("mmP"); evP = sem("evP")
        kwL = [sem("kwL0"), sem("kwL1")]; kwF = sem("kwF")
        csL = [sem("csL0"), sem("csL1")]; csF = sem("csF")
        kwW = [sem("kwW0"), sem("kwW1")]
        barS = sem("barS"); barC = sem("barC")
        vwL = sem("vwL"); vwF = sem("vwF")
        mmV = sem("mmV"); evV = sem("evV"); vstW = [sem("vstW0"), sem("vstW1")]
        kL = [sem("kL0"), sem("kL1")]
        vL = [sem("vL0"), sem("vL1")]
        kfT = sem("kfT"); kfE = sem("kfE")
        elV = sem("elV"); elA = sem("elA"); qfR = sem("qfR")
        mmM = sem("mmM"); evM = sem("evM"); mmMO = sem("mmMO"); evMO = sem("evMO")
        mmS = sem("mmS"); exS = sem("exS"); pvK = sem("pvK")
        cmb = sem("cmb"); cmf = sem("cmf"); cmE = sem("cmE")
        cmW = [sem("cmW0"), sem("cmW1")]
        woCL = [sem("woCL0"), sem("woCL1")]; woCF = sem("woCF"); woCH = sem("woCH")
        woL = [sem(f"woL{i}") for i in range(4)]; woF = sem("woF")
        gsb = sem("gsb"); mmG = sem("mmG"); evG = sem("evG"); vOn = sem("vOn")

        def lt_of(phase, it):
            return {"ffn1": 0, "attn": 4, "ffn2": 8}[phase] + it

        def xev(phase, it, tsub):
            base = {"ffn1": 0, "ln2": 16, "wo": 32, "ffn2": 48}[phase]
            return base + it * 4 + tsub

        def evb_ffn(tile):
            return tile * 16 + (64 if tile >= NTIL else 0)

        OS_B = {"ffn1": 0, "wo": 32, "ffn2": 64}

        ln1fm4 = ln1fm.ap().rearrange("p (dc s c) -> p dc s c", s=4, c=128)
        ln2fm = x2fm
        ln2fm4 = x2fm.ap().rearrange("p (dc tc c) -> p dc tc c", tc=16, c=128)

        do_attn = stage != "ffn1"
        do_ffn2 = stage == "full"

        block = e(nc.Block())

        @block.gpsimd
        def _(g):
            g.dma_start(out=ident[:, :], in_=ident_in).then_inc(sC, 16)
            g.dma_start(out=bupc[:, :], in_=bup_in).then_inc(sC, 16)
            g.dma_start(out=bdnr[:, :], in_=bdn_in).then_inc(sC, 16)
            g.dma_start(out=loraA[:, :], in_=ua_in).then_inc(sC, 16)
            g.dma_start(out=loraB[:, :], in_=ub_in).then_inc(sC, 16)
            g.dma_start(out=dnA[:, :], in_=da_in).then_inc(sC, 16)
            g.dma_start(out=dnB[:, :], in_=db_in).then_inc(sC, 16)
            g.dma_start(out=grow[:, :], in_=gate_in).then_inc(sC, 16)

            def cache_write(m):
                em.wge(g, dqW, 16 * m)
                if m < HC:
                    em.wge(g, dqE, 2 * (m + 1))
                    g.dma_start(out=wupT_c[m * 128:(m + 1) * 128, :],
                                in_=bufT[m % 2][:, :]).then_inc(dqW, 16)
                else:
                    km = m - HC
                    dcm, hqm = km % 16, km // 16
                    em.wge(g, dqE, 2 * (m + 1))
                    dst3 = wdnT_c[hqm * 2048:(hqm + 1) * 2048, dcm * 128:(dcm + 1) * 128]
                    dst3 = dst3.rearrange("(j r) c -> r j c", r=128)
                    src3 = bufT[m % 2].ap().rearrange("r (j c) -> r j c", c=128)
                    g.dma_start(out=dst3, in_=src3).then_inc(dqW, 16)

            for n in range(2 * HC):
                par = n % 2
                em.wge(g, dqV, n - 1)
                if n < HC:
                    srcq = wupq_in[n * 128:(n + 1) * 128, :]
                    srcs = wups_in[n * 128:(n + 1) * 128, :]
                else:
                    k = n - HC
                    dc, hq = k % 16, k // 16
                    srcq = wdnq_in[dc * 128:(dc + 1) * 128, hq * 2048:(hq + 1) * 2048]
                    srcs = wdns_in[dc * 128:(dc + 1) * 128, hq * 32:(hq + 1) * 32]
                for qt in range(4):
                    g.dma_start(out=bufQ[par][qt * 32:(qt + 1) * 32, :],
                                in_=srcq[qt * 32:(qt + 1) * 32, :]).then_inc(dqL[par], 16)
                g.dma_start(out=bufS[par][:, :], in_=srcs).then_inc(dqL[par], 16)
                if n >= 1:
                    cache_write(n - 1)
            cache_write(2 * HC - 1)

            if do_attn:
                em.wge(g, evB, 32)
                for mi, (w_in_, w_bf_) in enumerate(((wq_in, wq_bf), (wk_in, wk_bf),
                                                     (wv_in, wv_bf), (wo_in, wo_bf))):
                    em.wge(g, awc, 16 * mi)
                    g.dma_start(out=w_bf_, in_=w_in_).then_inc(awc, 16)
                gpid = nc.partition_id(engines=[mybir.EngineType.Pool])
                gslot = nc.snap(gpid % 2, engines=[mybir.EngineType.Pool])
                shk3g = sh_k.rearrange("(s r) t -> s r t", s=2)
                shv3g = sh_v.rearrange("(s r) c -> s r c", s=2)
                em.wge(g, ini, 1)
                g.dma_start(out=bar_i, in_=zrow[:, :]).then_inc(barS, 16)
                em.wge(g, kwW[0], 16 * 32)
                em.wge(g, kwW[1], 16 * 32)
                g.dma_start(out=shk3g[bass.ts(gslot, 1), :, :],
                            in_=kstg_k.unsqueeze(0)).then_inc(barS, 16)
                em.wge(g, vstW[0], 16 * 128)
                em.wge(g, vstW[1], 16 * 128)
                g.dma_start(out=shv3g[bass.ts(gslot, 1), :, :],
                            in_=kstg_v.unsqueeze(0)).then_inc(barS, 16)
                em.wge(g, barS, 48)
                g.collective_compute("AllReduce", ALU.add, replica_groups=PAIRS,
                                     ins=[bar_i.opt()], outs=[bar_o.opt()]).then_inc(barC, 1)

        @block.sync
        def _(s):
            pid = nc.partition_id(engines=[mybir.EngineType.SP])
            myslot = nc.snap(pid % 2, engines=[mybir.EngineType.SP])
            sh_k3 = sh_k.rearrange("(s r) t -> s r t", s=2)
            sh_v3 = sh_v.rearrange("(s r) c -> s r c", s=2)

            aw_list = ((wq_bf, wqT_c), (wk_bf, wkT_c), (wv_bf, wvT_c), (wo_bf, woT_c)) \
                if do_attn else ()
            aw_state = {"n": 0}

            def aw_step():
                st_ = aw_state["n"]
                if st_ > 64:
                    return
                if st_ < 64:
                    mi, jc = st_ // 16, st_ % 16
                    em.wge(s, awc, 16 * (mi + 1))
                    if st_ < 2:
                        em.wge(s, dqV, 2 * HC)
                    else:
                        em.wge(s, aww, 16 * (st_ - 1))
                    s.dma_start(out=awbuf[st_ % 2][:, :],
                                in_=aw_list[mi][0][:, jc * 128:(jc + 1) * 128],
                                transpose=True).then_inc(awt[st_ % 2], 16)
                if st_ >= 1:
                    pm = st_ - 1
                    mi, jc = pm // 16, pm % 16
                    em.wge(s, awt[pm % 2], 16 * (pm // 2 + 1))
                    s.dma_start(out=aw_list[mi][1][jc * 128:(jc + 1) * 128, :],
                                in_=awbuf[pm % 2][:, :]).then_inc(aww, 16)
                aw_state["n"] += 1

            def ffn_sync(phase, src, dst):
                p = {"ffn1": 0, "ffn2": 1}[phase]
                for it in range(NTIL):
                    tile = p * NTIL + it
                    for tsub in range(4):
                        if it > 0 or (phase == "ffn1" and tile > 0):
                            em.wge(s, evB, evb_ffn(tile - 1) + 12 + tsub + 1)
                        if phase == "ffn2" and it == 0:
                            em.wge(s, evB, 64 + 64)
                            em.wge(s, evP, 128)
                            em.wge(s, oS[0], 16 * 64)
                            em.wge(s, oS[1], 16 * 64)
                        r0 = it * TT + tsub * 128
                        s.dma_start(out=xt[tsub][:, :], in_=src[r0:r0 + 128, :]).then_inc(sXt[tsub], 16)
                    for hc in range(HC):
                        ev = tile * HC + hc
                        if do_attn and phase == "ffn1" and tile >= 2 and \
                                (hc % 2 == 0 or (tile == 3 and hc == 61)):
                            aw_step()
                        if tile == 0:
                            em.wge(s, dqW, 16 * (hc + 1))
                        else:
                            em.wge(s, dqW, 16 * HC)
                        if phase == "ffn2" and it == 0 and hc < 2:
                            em.wge(s, aww, 16 * 64)
                            em.wge(s, kwF, 32)
                        em.wge(s, wAF, ev - 1)
                        s.dma_start(out=wbufA[hc % 2][:, :],
                                    in_=wupT_c[hc * 128:(hc + 1) * 128, :]).then_inc(wAL[hc % 2], 16)
                    for dch in range(4):
                        for tsub in range(4):
                            em.wge(s, evB, evb_ffn(tile) + dch * 4 + tsub + 1)
                            r0 = it * TT + tsub * 128
                            s.dma_start(out=dst[r0:r0 + 128, dch * 512:(dch + 1) * 512],
                                        in_=outst[tsub % 2][:, :]).then_inc(oS[tsub % 2], 16)

            ffn1_dst = out_d if stage == "ffn1" else resid1
            ffn_sync("ffn1", x_in, ffn1_dst)
            if stage == "ffn1":
                em.wge(s, oS[0], 16 * 8 * NTIL)
                em.wge(s, oS[1], 16 * 8 * NTIL)
                return

            for it in range(NTIL):
                for tsub in range(4):
                    em.wge(s, oS[0], 128 * (it + 1))
                    em.wge(s, oS[1], 128 * (it + 1))
                    if it == 0:
                        em.wge(s, evB, 48 + 12 + tsub + 1)
                    else:
                        em.wge(s, lnN, 4 * (4 + it - 1) + 4)
                    r0 = it * TT + tsub * 128
                    s.dma_start(out=xt[tsub][:, :], in_=resid1[r0:r0 + 128, :]).then_inc(sXt[tsub], 16)

            def w_head_load(kl, cache, hh):
                em.wge(s, kwF, kl - 1)
                if kl < 2:
                    em.wge(s, aww, 16 * 64)
                for qtr in range(4):
                    src3 = cache[:, hh * 128:(hh + 1) * 128]
                    src3 = src3.rearrange("(jc p) c -> p jc c", p=128)[:, qtr * 4:(qtr + 1) * 4, :]
                    dst3 = wbufA[kl % 2].ap().rearrange("p (jc c) -> p jc c", c=128)[
                        :, qtr * 4:(qtr + 1) * 4, :]
                    s.dma_start(out=dst3, in_=src3).then_inc(kwL[kl % 2], 16)

            for pc in range(8):
                for hi in range(2):
                    kl = pc * 2 + hi
                    w_head_load(kl, wkT_c, pc + 8 * hi)
                for tc in range(4):
                    u = pc * 4 + tc
                    for hi in range(2):
                        hh = pc + 8 * hi
                        em.wge(s, evP, 2 * u + 2)
                        s.dma_start(out=kstg_k[hh * 128:(hh + 1) * 128, tc * 512:(tc + 1) * 512],
                                    in_=kst[hi][:, :]).then_inc(kwW[hi], 16)

            for hg in range(4):
                em.wge(s, vwF, hg)
                for qtr in range(4):
                    src3 = wvT_c[:, hg * 512:(hg + 1) * 512]
                    src3 = src3.rearrange("(jc p) c -> p jc c", p=128)[:, qtr * 4:(qtr + 1) * 4, :]
                    dst3 = wvbuf.ap().rearrange("p (jc c) -> p jc c", c=512)[:, qtr * 4:(qtr + 1) * 4, :]
                    s.dma_start(out=dst3, in_=src3).then_inc(vwL, 16)
                for tsub in range(16):
                    vidx = hg * 16 + tsub
                    em.wge(s, evV, vidx + 1)
                    for j in range(4):
                        hh = hg * 4 + j
                        s.dma_start(out=kstg_v[hh * T + tsub * 128: hh * T + (tsub + 1) * 128, :],
                                    in_=vst[vidx % 2][:, j * 128:(j + 1) * 128]
                                    ).then_inc(vstW[vidx % 2], 16)

            def cmb_store(h):
                hhp = HORDER[h]
                for g2 in range(2):
                    em.wge(s, cmE, h * 2 + g2 + 1)
                    s.dma_start(out=cmb_d[hhp * 128:(hhp + 1) * 128, g2 * 1024:(g2 + 1) * 1024],
                                in_=cmbOut[g2][:, :]).then_inc(cmW[g2], 16)

            for h_idx in range(NH):
                hh = HORDER[h_idx]
                par = h_idx % 2
                em.wge(s, barC, 1)
                if h_idx < 2:
                    em.wge(s, aww, 16 * 64)
                em.wge(s, mmS, (h_idx - 1) * 128)
                for slot in range(2):
                    s.dma_start(out=k_buf[par][:, slot * 2048:(slot + 1) * 2048],
                                in_=sh_k[(slot * 16 + hh) * 128:(slot * 16 + hh) * 128 + 128, :]
                                ).then_inc(kL[par], 16)
                em.wge(s, pvK, (h_idx - 1) * 128)
                for slot in range(2):
                    for ktg in range(4):
                        rb = (slot * 16 + hh) * T + ktg * 512
                        src3 = sh_v[rb:rb + 512, :].rearrange("(kt t) c -> t kt c", t=128)
                        base_kt = slot * 16 + ktg * 4
                        dst3 = v_buf[par].ap().rearrange("p (kt c) -> p kt c", c=130)[
                            :, base_kt:base_kt + 4, 0:128]
                        s.dma_start(out=dst3, in_=src3).then_inc(vL[par], 16)
                if h_idx == 0:
                    for hi in range(2):
                        w_head_load(16 + hi, wqT_c, HORDER[hi])
                if h_idx % 2 == 1 and h_idx < NH - 1:
                    for hi in range(2):
                        kl = 17 + h_idx + hi
                        w_head_load(kl, wqT_c, HORDER[h_idx + 1 + hi])
                if h_idx > 0:
                    cmb_store(h_idx - 1)
            cmb_store(NH - 1)

            wo_dst = out_d if stage == "attn" else resid2
            for it in range(NTIL):
                for jc in range(DC):
                    if jc == 0:
                        em.wge(s, woCH, max(0, 2 * it - 1))
                        if it == 0:
                            em.wge(s, cmW[0], 16 * NH)
                            em.wge(s, cmW[1], 16 * NH)
                    elif jc == 8:
                        em.wge(s, woCH, 2 * it)
                    s.dma_start(out=woCbuf[:, jc * 512:(jc + 1) * 512],
                                in_=cmb_d[jc * 128:(jc + 1) * 128, it * 512:(it + 1) * 512]
                                ).then_inc(woCL[jc // 8], 16)
                for tsub in range(4):
                    if it > 0:
                        em.wge(s, evB, 64 + (it - 1) * 16 + 12 + tsub + 1)
                    else:
                        em.wge(s, evP, 128)
                        em.wge(s, oS[0], 16 * 32)
                        em.wge(s, oS[1], 16 * 32)
                    r0 = it * TT + tsub * 128
                    s.dma_start(out=xt[tsub][:, :], in_=resid1[r0:r0 + 128, :]).then_inc(sXt[tsub], 16)
                def wo_stores(ic):
                    for tsub in range(4):
                        em.wge(s, evB, 64 + (it * 4 + ic) * 4 + tsub + 1)
                        r0 = it * TT + tsub * 128
                        s.dma_start(out=wo_dst[r0:r0 + 128, ic * 512:(ic + 1) * 512],
                                    in_=outst[tsub % 2][:, :]).then_inc(oS[tsub % 2], 16)

                for ic in range(4):
                    if ic > 0:
                        wo_stores(ic - 1)
                    for jc in range(DC):
                        wev = (it * 4 + ic) * DC + jc
                        em.wge(s, woF, wev - 3)
                        s.dma_start(out=wbufB[jc % 4][:, :],
                                    in_=woT_c[jc * 128:(jc + 1) * 128, ic * 512:(ic + 1) * 512]
                                    ).then_inc(woL[jc % 4], 16)
                wo_stores(3)

            if stage == "attn":
                em.wge(s, oS[0], 16 * (OS_B["wo"] + 32))
                em.wge(s, oS[1], 16 * (OS_B["wo"] + 32))
                return

            ffn_sync("ffn2", resid2, out_d)
            em.wge(s, oS[0], 16 * (OS_B["ffn2"] + 32))
            em.wge(s, oS[1], 16 * (OS_B["ffn2"] + 32))

        @block.scalar
        def _(sc):
            sctr = {"n": 0}

            def sf(ins):
                sctr["n"] += 1
                ins.then_inc(ssy, 1)
                em.wge(sc, ssy, sctr["n"])

            em.wge(sc, sC, 128)
            em.wge(sc, ini, 1)
            sf(sc.activation(grow[0:1, :], grow[0:1, :], AF.Sigmoid))

            def ln_scalar(phase, it):
                lt = lt_of(phase, it)
                xph = {"ffn1": "ffn1", "ffn2": "ffn2", "attn": "ln2"}[phase]
                for tsub in range(4):
                    em.wge(sc, sXt[tsub], 16 * ((xev(xph, it, 0) // 4) + 1))
                    em.wge(sc, lnR, 8 * lt + 2 * tsub)
                    sc.activation(sq[:, :], xt[tsub][:, :], AF.Square).then_inc(lnQ, 1)
                em.wge(sc, lnV, lt + 1)
                sf(sc.activation(stat[:, 8:12], stat[:, 4:8], AF.Ln, bias=cstc[:, 0:1]))
                sc.activation(stat[:, 12:16], stat[:, 8:12], AF.Exp, scale=-0.5).then_inc(lnST, 1)

            def dq_evac(m):
                for grp in range(2):
                    em.wge(sc, dqT, 2 * m + grp + 1)
                    em.wge(sc, dqW, 16 * (m - 1))
                    sc.activation(bufT[m % 2][:, grp * 1024:(grp + 1) * 1024],
                                  psDQ, AF.Identity).then_inc(dqE, 1)

            def ffn_scalar(phase):
                p = {"ffn1": 0, "ffn2": 1}[phase]
                for it in range(NTIL):
                    tile = p * NTIL + it
                    if phase == "ffn2" and it == 0:
                        em.wge(sc, cmb, 4 * NH)
                    ln_scalar(phase, it)
                    if tile > 0:
                        em.wge(sc, wBF, tile * 4 * HC)
                    if phase == "ffn2" and it == 0:
                        em.wge(sc, mmP, 128)
                    if tile == 0:
                        for m in range(21):
                            dq_evac(m)
                    for hc in range(HC):
                        ev = tile * HC + hc
                        em.wge(sc, mmA, ev + 1)
                        sc.activation(x2fm[:, hc * TT:(hc + 1) * TT], psA[hc % 2],
                                      AF.Relu, bias=bupc[:, hc:hc + 1]).then_inc(evA, 1)
                        if tile == 0:
                            dq_evac(hc + 21)
                    if tile == 0:
                        for m in range(HC + 21, 2 * HC):
                            dq_evac(m)
                    for dch in range(4):
                        for hc in range(HC):
                            wev = (tile * 4 + dch) * HC + hc
                            if tile == 0:
                                em.wge(sc, dqW,
                                       16 * (HC + (hc // 16) * 16 + dch * 4 + 4))
                            else:
                                em.wge(sc, dqW, 16 * 2 * HC)
                            em.wge(sc, wBF, wev - 3)
                            s0 = hc * 128
                            sc.dma_start(out=wbufB[hc % 4][:, :],
                                         in_=wdnT_c[s0:s0 + 128, dch * 512:(dch + 1) * 512]
                                         ).then_inc(wBL[hc % 4], 16)

            ffn_scalar("ffn1")
            if stage == "ffn1":
                return

            for it in range(NTIL):
                ln_scalar("attn", it)

            def cs_load_u(u, pc, tc):
                if u < 2:
                    em.wge(sc, lnN, 4 * 8)
                else:
                    em.wge(sc, evP, 2 * (u - 2) + 2)
                sl = u % 2
                sc.dma_start(out=cs2c[sl][:, :],
                             in_=cos_in[pc * 128:(pc + 1) * 128, tc * 512:(tc + 1) * 512]
                             ).then_inc(csL[sl], 16)
                sc.dma_start(out=cs2s[sl][:, :],
                             in_=sin_in[pc * 128:(pc + 1) * 128, tc * 512:(tc + 1) * 512]
                             ).then_inc(csL[sl], 16)

            for u in range(32):
                cs_load_u(u, u // 4, u % 4)
            for hg in range(4):
                for tsub in range(16):
                    vidx = hg * 16 + tsub
                    em.wge(sc, mmV, vidx + 1)
                    em.wge(sc, vstW[vidx % 2], 64 * (vidx // 2))
                    sc.activation(vst[vidx % 2][:, :], psV[vidx % 2], AF.Identity).then_inc(evV, 1)
            def elu_exp(gi):
                em.wge(sc, elV, gi + 1)
                sc.activation(tmpAB[:, :], tmpAB[:, :], AF.Exp).then_inc(elA, 1)

            for tc in range(4):
                cs_load_u(32 + tc, 0, tc)
            for g2 in range(4):
                elu_exp(g2)
            elu_exp(4)
            elu_exp(5)
            for h_idx in range(NH):
                if h_idx % 2 == 0 and h_idx // 2 + 1 <= 7:
                    p_ = h_idx // 2 + 1
                    for tc in range(2):
                        cs_load_u(32 + p_ * 4 + tc, p_, tc)
                for qc in range(4):
                    for kt in range(32):
                        si = sidx(h_idx, qc, kt)
                        em.wge(sc, mmS, si + 1)
                        if si >= 2:
                            em.wge(sc, pvK, si - 1)
                        sc.activation(ptb[kt % 2][:, 0:512], psS[kt % 2], AF.Exp,
                                      scale=INV_SQRT_HD, bias=cstc[:, 1:2]).then_inc(exS, 1)
                        if kt == 18 and h_idx < NH - 1:
                            elu_exp((h_idx + 1) * 6 + qc)
                        if qc == 3 and kt in (24, 28) and h_idx < NH - 1:
                            elu_exp((h_idx + 1) * 6 + 4 + (1 if kt == 28 else 0))
                        if kt == 20 and qc in (1, 2) and h_idx % 2 == 1 and h_idx < NH - 1:
                            p_ = (h_idx + 1) // 2
                            cs_load_u(32 + p_ * 4 + qc + 1, p_, qc + 1)

            if stage == "attn":
                return
            ffn_scalar("ffn2")

        @block.tensor
        def _(t):
            em.wge(t, sC, 128)
            em.wge(t, ini, 1)

            def dq_tr(n):
                em.wge(t, dqV, n + 1)
                for grp in range(2):
                    G = 2 * n + grp
                    em.wge(t, dqE, G)
                    for j in range(8):
                        dcc = grp * 8 + j
                        ins = t.transpose(psDQ3[:, j, :],
                                          bufD[n % 2][:, dcc * 128:(dcc + 1) * 128],
                                          ident[:, :])
                    ins.then_inc(dqT, 1)

            def ln_tensor(phase, it):
                lt = lt_of(phase, it)
                for tsub in range(4):
                    em.wge(t, lnN, 4 * lt + tsub + 1)
                    for grp in range(2):
                        gidx = (4 * lt + tsub) * 2 + grp
                        em.wge(t, lnE, gidx)
                        for j in range(8):
                            dcc = grp * 8 + j
                            ins = t.transpose(psLN3[:, j, :],
                                              lntm[tsub % 2][:, dcc * 128:(dcc + 1) * 128],
                                              ident[:, :])
                        ins.then_inc(lnT, 1)

            def ffn_tensor(phase):
                p = {"ffn1": 0, "ffn2": 1}[phase]
                for it in range(NTIL):
                    tile = p * NTIL + it
                    ln_tensor(phase, it)
                    if it > 0 or (phase == "ffn1" and tile > 0):
                        em.wge(t, evB, evb_ffn(tile - 1) + 16)
                    if phase == "ffn2" and it == 0:
                        em.wge(t, evB, 64 + 64)
                    em.wge(t, lnE, (4 * lt_of(phase, it) + 4) * 2)
                    for dcc in range(DC):
                        ins = t.matmul(psXA, loraA[:, dcc * R:(dcc + 1) * R],
                                       ln1fm[:, dcc * TT:(dcc + 1) * TT],
                                       start=(dcc == 0), stop=(dcc == DC - 1))
                    ins.then_inc(mmXA, 1)
                    em.wge(t, eXA, tile + 1)
                    if tile == 0:
                        for n in range(20):
                            dq_tr(n)
                    for hc in range(HC):
                        ev = tile * HC + hc
                        if tile == 0 and hc + 20 < HC + 20:
                            dq_tr(hc + 20)
                        em.wge(t, wAL[hc % 2], 16 * (ev // 2 + 1))
                        em.wge(t, evA, ev - 1)
                        for dcc in range(DC):
                            ins = t.matmul(psA[hc % 2], wbufA[hc % 2][:, dcc * 128:(dcc + 1) * 128],
                                           ln1fm[:, dcc * TT:(dcc + 1) * TT],
                                           start=(dcc == 0), stop=False)
                        ins.then_inc(wAF, 1)
                        ins = t.matmul(psA[hc % 2], loraB[:, hc * 128:(hc + 1) * 128], xa_sb[:, :],
                                       start=False, stop=True)
                        ins.then_inc(mmA, 1)
                        hx = hc - 2
                        if 0 <= hx < HC - 2:
                            em.wge(t, evA, tile * HC + hx + 1)
                            t.matmul(psXA, dnA[:, hx * R:(hx + 1) * R],
                                     x2fm[:, hx * TT:(hx + 1) * TT],
                                     start=(hx == 0), stop=False)
                    for hx in (HC - 2, HC - 1):
                        em.wge(t, evA, tile * HC + hx + 1)
                        ins = t.matmul(psXA, dnA[:, hx * R:(hx + 1) * R],
                                       x2fm[:, hx * TT:(hx + 1) * TT],
                                       start=False, stop=(hx == HC - 1))
                    ins.then_inc(mmXDA, 1)
                    if tile == 0:
                        for n in range(HC + 20, 2 * HC):
                            dq_tr(n)
                    em.wge(t, eXDA, tile + 1)
                    for dch in range(4):
                        if dch > 0:
                            em.wge(t, evB, evb_ffn(tile) + dch * 4)
                        elif it > 0 or (phase == "ffn1" and tile > 0):
                            em.wge(t, evB, evb_ffn(tile - 1) + 16)
                        elif phase == "ffn2":
                            em.wge(t, evB, 64 + 64)
                        for tsub in range(4):
                            t.matmul(psB[tsub], ones1[:, :], bdnr[:, dch * 512:(dch + 1) * 512],
                                     start=True, stop=False)
                            t.matmul(psB[tsub], xda_sb[:, tsub * 128:(tsub + 1) * 128],
                                     dnB[:, dch * 512:(dch + 1) * 512],
                                     start=False, stop=False)
                        for hc in range(HC):
                            wev = (tile * 4 + dch) * HC + hc
                            em.wge(t, wBL[hc % 4], 16 * (wev // 4 + 1))
                            for tsub in range(4):
                                ins = t.matmul(psB[tsub],
                                               x2fm[:, hc * TT + tsub * 128: hc * TT + (tsub + 1) * 128],
                                               wbufB[hc % 4][:, :],
                                               start=False, stop=(hc == HC - 1))
                            ins.then_inc(wBF, 1)

            ffn_tensor("ffn1")
            if stage == "ffn1":
                return

            em.wge(t, gsb, 1)
            t.matmul(psG, ones1[:, :], growb[0:1, :], start=True, stop=True).then_inc(mmG, 1)
            for it in range(NTIL):
                if it == 0:
                    em.wge(t, evG, 1)
                ln_tensor("attn", it)

            def proj_fill(u, kl, tc, which):
                em.wge(t, evP, 2 * u + which)
                em.wge(t, kwL[kl % 2], 64 * (kl // 2 + 1))
                for jc in range(DC):
                    ins = t.matmul(psPJ, wbufA[kl % 2][:, jc * 128:(jc + 1) * 128],
                                   ln2fm[:, jc * 2048 + tc * 512: jc * 2048 + tc * 512 + 512],
                                   start=(jc == 0), stop=(jc == DC - 1))
                ins.then_inc(mmP, 1)

            def free_pair_bufs(kl_lo, kl_hi, u_last):
                em.wge(t, evP, 2 * u_last + 2)
                ins = t.matmul(psPJ[:, 0:128], wbufA[kl_lo % 2][:, 0:128], ident[:, :],
                               start=True, stop=True)
                ins.then_inc(kwF, 1)
                ins = t.matmul(psPJ[:, 0:128], wbufA[kl_hi % 2][:, 0:128], ident[:, :],
                               start=True, stop=True)
                ins.then_inc(kwF, 1)

            for pc in range(8):
                kl_lo, kl_hi = pc * 2, pc * 2 + 1
                for tc in range(4):
                    u = pc * 4 + tc
                    if pc == 0:
                        em.wge(t, lnE, (4 * (4 + tc) + 4) * 2)
                    proj_fill(u, kl_lo, tc, 0)
                    proj_fill(u, kl_hi, tc, 1)
                free_pair_bufs(kl_lo, kl_hi, pc * 4 + 3)

            for hg in range(4):
                em.wge(t, vwL, 64 * (hg + 1))
                if hg == 0:
                    em.wge(t, evP, 2 * 32)
                for tsub in range(16):
                    vidx = hg * 16 + tsub
                    em.wge(t, evV, vidx - 1)
                    for jc in range(DC):
                        ins = t.matmul(psV[vidx % 2],
                                       ln2fm4[:, jc, tsub, :],
                                       wvbuf[:, jc * 512:(jc + 1) * 512],
                                       start=(jc == 0), stop=(jc == DC - 1))
                    ins.then_inc(mmV, 1)
                ins.then_inc(vwF, 1) if False else t.matmul(
                    psPJ[:, 0:128], wvbuf[:, 0:128], ident[:, :], start=True, stop=True
                ).then_inc(vwF, 1)

            for tc in range(4):
                proj_fill(32 + tc, 16, tc, 0)
                proj_fill(32 + tc, 17, tc, 1)
            free_pair_bufs(16, 17, 35)
            em.wge(t, kL[0], 32)
            em.wge(t, evP, 2 * 35 + 2)
            for g2 in range(4):
                em.wge(t, kfE, g2)
                for j in range(8):
                    st_ = g2 * 8 + j
                    ins = t.transpose(psDQ3[:, j, :],
                                      k_buf[0][:, st_ * 128:(st_ + 1) * 128],
                                      ident[:, :])
                ins.then_inc(kfT, 1)

            def mem_mm(h):
                mpar = h % 2
                em.wge(t, vL[mpar], 128 * (h // 2 + 1))
                em.wge(t, vOn, 1)
                em.wge(t, kfE, h * 4 + 4)
                if mpar == 0 and h > 0:
                    em.wge(t, evP, 2 * (32 + (h // 2) * 4 + 3) + 2)
                for kt in range(32):
                    ins = t.matmul(psMEM[:, 0:129], kf_tm[:, kt * 128:(kt + 1) * 128],
                                   v_buf[mpar][:, kt * 130:kt * 130 + 129],
                                   start=(kt == 0), stop=(kt == 31))
                ins.then_inc(mmM, 1)

            mem_mm(0)

            for h_idx in range(NH):
                hh = HORDER[h_idx]
                par = h_idx % 2
                pp = (h_idx // 2) % 2
                if h_idx == 0:
                    em.wge(t, evV, 64)
                if par == 0 and h_idx > 0:
                    em.wge(t, evP, 2 * (32 + (h_idx // 2) * 4 + 3) + 2)
                s_mm0 = None

                def s_mm(qc, kt):
                    si = sidx(h_idx, qc, kt)
                    if si >= 2:
                        em.wge(t, exS, si - 1)
                    ins = t.matmul(psS[kt % 2],
                                   k_buf[par][:, kt * 128:(kt + 1) * 128],
                                   qrp4[pp][par][:, qc * 512:(qc + 1) * 512],
                                   start=True, stop=True)
                    ins.then_inc(mmS, 1)

                s_mm(0, 0)
                s_mm(0, 1)
                em.wge(t, evM, h_idx + 1)
                em.wge(t, qfR, 2 * (h_idx + 1))
                for w in range(2):
                    if h_idx * 2 + w > 0:
                        em.wge(t, evMO, h_idx * 2 + w)
                    if w == 0 and h_idx > 0:
                        em.wge(t, cmb, h_idx * 4)
                    for j in range(8):
                        qsub = w * 8 + j
                        ins = t.matmul(psMO[j][:, 0:129], q_feat[:, qsub * 128:(qsub + 1) * 128],
                                       memSB[:, 0:129], start=True, stop=True)
                    ins.then_inc(mmMO, 1)
                def comb_flip(qc):
                    em.wge(t, cmb, h_idx * 4 + qc + 1)
                    em.wge(t, cmE, h_idx * 2 + qc // 2)
                    for j in range(4):
                        ins = t.transpose(psLN3[:, (qc % 2) * 4 + j, :],
                                          cmbTM[qc % 2][:, j * 128:(j + 1) * 128],
                                          ident[:, :])
                    ins.then_inc(cmf, 1)

                nh1 = h_idx + 1
                np_ = nh1 // 2

                for qc in range(4):
                    for kt in range(32):
                        si = sidx(h_idx, qc, kt)
                        if 0 < kt < 31:
                            s_mm(qc, kt + 1)
                        elif kt == 31 and qc < 3:
                            s_mm(qc + 1, 0)
                            s_mm(qc + 1, 1)
                        em.wge(t, exS, si + 1)
                        if kt == 0:
                            em.wge(t, evMO, h_idx * 2 + 2)
                            em.wge(t, cmb, h_idx * 4 + qc)
                        for j in range(4):
                            ins = t.matmul(psPV[j][:, 0:129],
                                           ptb[kt % 2][:, j * 128:(j + 1) * 128],
                                           v_buf[par][:, kt * 130:kt * 130 + 129],
                                           start=(kt == 0), stop=(kt == 31))
                        ins.then_inc(pvK, 1)
                        if kt == 2 and par == 1 and nh1 < NH:
                            em.wge(t, evM, h_idx + 1)
                            if qc == 0:
                                em.wge(t, qfR, max(0, 2 * (2 * np_ - 2)))
                            else:
                                em.wge(t, kfE, 4 * nh1 + qc)
                            proj_fill(32 + np_ * 4 + qc, 16 + 2 * np_, qc, 0)
                        if kt == 6 and par == 1 and nh1 < NH:
                            proj_fill(32 + np_ * 4 + qc, 17 + 2 * np_, qc, 1)
                        if kt == 16 and qc == 3 and par == 1 and nh1 < NH:
                            free_pair_bufs(16 + 2 * np_, 17 + 2 * np_, 32 + np_ * 4 + 3)
                        if nh1 < NH and ((kt == 14 and qc < 3) or (kt == 18 and qc == 3)):
                            g2 = qc
                            em.wge(t, kfE, 4 * nh1 + g2)
                            if g2 == 0:
                                em.wge(t, kL[nh1 % 2], 32 * (nh1 // 2 + 1))
                                em.wge(t, evM, h_idx + 1)
                            if par == 1:
                                em.wge(t, evP, 2 * (32 + np_ * 4 + g2) + 2)
                            for j in range(8):
                                st_ = g2 * 8 + j
                                ins = t.transpose(psDQ3[:, j, :],
                                                  k_buf[nh1 % 2][:, st_ * 128:(st_ + 1) * 128],
                                                  ident[:, :])
                            ins.then_inc(kfT, 1)
                    if qc >= 2:
                        comb_flip(qc - 2)
                if h_idx < NH - 1:
                    mem_mm(h_idx + 1)
                comb_flip(2)
                comb_flip(3)

            for it in range(NTIL):
                for ic in range(4):
                    if it * 4 + ic > 0:
                        em.wge(t, evB, 64 + (it * 4 + ic - 1) * 4 + 4)
                    else:
                        em.wge(t, pvK, 128 * NH)
                        em.wge(t, cmb, 4 * NH)
                        em.wge(t, evM, NH)
                    for jc in range(DC):
                        wev = (it * 4 + ic) * DC + jc
                        if ic == 0 and jc in (0, 8):
                            em.wge(t, woCL[jc // 8], 16 * 8 * (it + 1))
                        em.wge(t, woL[jc % 4], 16 * (wev // 4 + 1))
                        for tsub in range(4):
                            ins = t.matmul(psB[tsub],
                                           woCbuf[:, jc * 512 + tsub * 128: jc * 512 + (tsub + 1) * 128],
                                           wbufB[jc % 4][:, :],
                                           start=(jc == 0), stop=(jc == DC - 1))
                        ins.then_inc(woF, 1)
                        if ic == 3 and jc in (7, 15):
                            ins = t.matmul(psPJ[:, 0:128], woCbuf[:, jc * 512:jc * 512 + 128],
                                           ident[:, :], start=True, stop=True)
                            ins.then_inc(woCH, 1)
                ins = t.matmul(psPJ[:, 0:128], ident[:, :], ident[:, 0:128], start=True, stop=True)
                ins.then_inc(woCF, 1)

            if stage == "attn":
                return
            ffn_tensor("ffn2")

        @block.vector
        def _(v):
            vctr = {"n": 0}

            def vf(ins):
                vctr["n"] += 1
                ins.then_inc(vsy, 1)
                em.wge(v, vsy, vctr["n"])

            em.wge(v, sC, 128)
            v.memset(ones1[:, :], 1.0)
            v.memset(zrow[:, :], 0.0)
            v.memset(zb512[:, :], 0.0)
            v.memset(cstc[:, 0:1], EPS)
            v.memset(cstc[:, 1:2], EXP_BIAS).then_inc(ini, 1)

            def dq_vector():
                for n in range(2 * HC):
                    par = n % 2
                    em.wge(v, dqL[par], 80 * (n // 2 + 1))
                    em.wge(v, dqT, 2 * n - 2)
                    sview = bufS[par][:, :].unsqueeze(2).broadcast_to([128, 32, 64])
                    qview = bufQ[par][:, :].rearrange("p (b i) -> p b i", i=64)
                    dview = bufD[par][:, :].rearrange("p (b i) -> p b i", i=64)
                    v.scalar_tensor_tensor(dview, qview, -7.5, sview, ALU.add, ALU.mult).then_inc(dqV, 1)

            def ln_vector(phase, it, dst_fn, mm_gate=None):
                lt = lt_of(phase, it)
                xph = {"ffn1": "ffn1", "ffn2": "ffn2", "attn": "ln2"}[phase]
                for tsub in range(4):
                    em.wge(v, sXt[tsub], 16 * ((xev(xph, it, 0) // 4) + 1))
                    v.reduce_sum(stat[:, tsub:tsub + 1], xt[tsub][:, :], axis=AX.X).then_inc(lnR, 1)
                    em.wge(v, lnQ, 4 * lt + tsub + 1)
                    v.reduce_sum(stat[:, 4 + tsub:5 + tsub], sq[:, :], axis=AX.X).then_inc(lnR, 1)
                em.wge(v, lnR, 8 * lt + 8)
                vf(v.tensor_scalar(stat[:, 0:4], stat[:, 0:4], INV_D, None, ALU.mult))
                vf(v.tensor_scalar(stat[:, 4:8], stat[:, 4:8], INV_D, None, ALU.mult))
                vf(v.tensor_mul(stat[:, 8:12], stat[:, 0:4], stat[:, 0:4]))
                v.tensor_sub(stat[:, 4:8], stat[:, 4:8], stat[:, 8:12]).then_inc(lnV, 1)
                em.wge(v, lnST, lt + 1)
                if mm_gate is not None:
                    mm_gate()

                def emit_evacs(ts):
                    for grp in range(2):
                        gidx = (4 * lt + ts) * 2 + grp
                        em.wge(v, lnT, gidx + 1)
                        v.tensor_copy(dst_fn(grp, ts), psLN3[:, 0:8, :]).then_inc(lnE, 1)

                for tsub in range(4):
                    if tsub >= 2:
                        em.wge(v, lnT, (4 * lt + tsub - 1) * 2)
                    v.tensor_scalar(lntm[tsub % 2][:, :], xt[tsub][:, :],
                                    stat[:, tsub:tsub + 1], stat[:, 12 + tsub:13 + tsub],
                                    ALU.subtract, ALU.mult).then_inc(lnN, 1)
                    if tsub >= 1:
                        emit_evacs(tsub - 1)
                em.wge(v, lnN, 4 * lt + 4)
                emit_evacs(3)

            def ffn_vector(phase):
                p = {"ffn1": 0, "ffn2": 1}[phase]
                for it in range(NTIL):
                    tile = p * NTIL + it

                    def gate():
                        if tile > 0:
                            em.wge(v, mmA, tile * HC)
                            em.wge(v, mmXA, tile)
                        if phase == "ffn2" and it == 0:
                            em.wge(v, mmP, 128)
                            em.wge(v, mmS, 128 * NH)

                    def dst_fn(grp, tsub):
                        return ln1fm4[:, grp * 8:(grp + 1) * 8, tsub, :]

                    if phase == "ffn2" and it == 0:
                        em.wge(v, woCF, NTIL)
                        em.wge(v, qfR, 2 * NH)
                    ln_vector(phase, it, dst_fn, mm_gate=gate)
                    em.wge(v, mmXA, tile + 1)
                    v.tensor_copy(xa_sb[:, :], psXA).then_inc(eXA, 1)
                    if tile == 0:
                        dq_vector()
                    em.wge(v, mmXDA, tile + 1)
                    v.tensor_copy(xda_sb[:, :], psXA).then_inc(eXDA, 1)
                    for dch in range(4):
                        em.wge(v, wBF, (tile * 4 + dch + 1) * HC)
                        for tsub in range(4):
                            em.wge(v, oS[tsub % 2],
                                   16 * (OS_B[phase] + (it * 4 + dch) * 2 + tsub // 2))
                            v.scalar_tensor_tensor(outst[tsub % 2][:, :], psB[tsub], 0.5,
                                                   xt[tsub][:, dch * 512:(dch + 1) * 512],
                                                   ALU.mult, ALU.add).then_inc(evB, 1)

            ffn_vector("ffn1")
            if stage == "ffn1":
                return

            em.wge(v, ssy, 1)
            v.tensor_copy(growb[0:1, :], grow[0:1, :]).then_inc(gsb, 1)
            em.wge(v, mmG, 1)
            ins = v.tensor_copy(gcol[:, :], psG)
            ins.then_inc(evG, 1)
            em.wge(v, evG, 1)
            vf(v.tensor_scalar(gicol[:, :], gcol[:, :], -1.0, 1.0, ALU.mult, ALU.add))

            for it in range(NTIL):
                def gate2():
                    em.wge(v, wBF, 4 * NTIL * HC)

                def dst2(grp, tsub, _it=it):
                    return ln2fm4[:, grp * 8:(grp + 1) * 8, _it * 4 + tsub, :]

                ln_vector("attn", it, dst2, mm_gate=gate2)

            em.wge(v, dqW, 16 * 2 * HC)
            for par in range(2):
                ones_view = v_buf[par].ap().rearrange("p (kt c) -> p kt c", c=130)[:, :, 128:129]
                ins = v.memset(ones_view, 1.0)
            ins.then_inc(vOn, 1)

            for pc in range(8):
                for tc in range(4):
                    u = pc * 4 + tc
                    em.wge(v, csL[u % 2], 32 * (u // 2 + 1))
                    em.wge(v, mmP, 2 * u + 1)
                    ins = v.tensor_copy(klo_sb[:, :], psPJ)
                    ins.then_inc(evP, 1)
                    em.wge(v, evP, 2 * u + 1)
                    em.wge(v, mmP, 2 * u + 2)
                    csl = cs2c[u % 2][:, :]
                    ssl = cs2s[u % 2][:, :]
                    em.wge(v, kwW[0], 16 * u)
                    em.wge(v, kwW[1], 16 * u)
                    vf(v.tensor_mul(tmpAB[:, 0:512], klo_sb[:, :], csl))
                    vf(v.tensor_mul(tmpAB[:, 512:1024], psPJ, ssl))
                    vf(v.tensor_sub(kst[0][:, :], tmpAB[:, 0:512], tmpAB[:, 512:1024]))
                    vf(v.tensor_mul(tmpAB[:, 0:512], psPJ, csl))
                    vf(v.tensor_mul(tmpAB[:, 512:1024], klo_sb[:, :], ssl))
                    ins = v.tensor_add(kst[1][:, :], tmpAB[:, 0:512], tmpAB[:, 512:1024])
                    ins.then_inc(evP, 1)
                    em.wge(v, evP, 2 * u + 2)
                v.engine_nop().then_inc(csF, 1)

            def q_rope_tc(p, tc, chain):
                em.wge(v, elA, chain)
                u = 32 + p * 4 + tc
                em.wge(v, csL[u % 2], 32 * (u // 2 + 1))
                em.wge(v, mmP, 2 * u + 1)
                ins = v.tensor_copy(klo_sb[:, :], psPJ)
                ins.then_inc(evP, 1)
                em.wge(v, evP, 2 * u + 1)
                em.wge(v, mmP, 2 * u + 2)
                csl = cs2c[u % 2][:, :]
                ssl = cs2s[u % 2][:, :]
                vf(v.tensor_mul(tmpAB[:, 0:512], klo_sb[:, :], csl))
                vf(v.tensor_mul(tmpAB[:, 512:1024], psPJ, ssl))
                vf(v.tensor_sub(qrp4[p % 2][0][:, tc * 512:(tc + 1) * 512],
                                tmpAB[:, 0:512], tmpAB[:, 512:1024]))
                vf(v.tensor_mul(tmpAB[:, 0:512], psPJ, csl))
                vf(v.tensor_mul(tmpAB[:, 512:1024], klo_sb[:, :], ssl))
                ins = v.tensor_add(qrp4[p % 2][1][:, tc * 512:(tc + 1) * 512],
                                   tmpAB[:, 0:512], tmpAB[:, 512:1024])
                ins.then_inc(evP, 1)
                em.wge(v, evP, 2 * u + 2)
                if tc == 3:
                    v.engine_nop().then_inc(csF, 1)

            def kelu(h, g):
                gi = h * 6 + g
                em.wge(v, kfT, h * 4 + g + 1)
                if gi > 0:
                    em.wge(v, elA, gi)
                ins = v.tensor_scalar(tmpAB[:, :], psDQ, 0.0, None, ALU.min)
                ins.then_inc(elV, 1)
                em.wge(v, elA, gi + 1)
                ins = v.scalar_tensor_tensor(kf_tm[:, g * 1024:(g + 1) * 1024],
                                             psDQ, 0.0, tmpAB[:, :],
                                             ALU.max, ALU.add)
                ins.then_inc(kfE, 1)
                em.wge(v, kfE, h * 4 + g + 1)

            def q_elu(h):
                hpar = h % 2
                hpp = (h // 2) % 2
                em.wge(v, evP, 2 * (32 + (h // 2) * 4 + 3) + 2)
                for hf in range(2):
                    gi = h * 6 + 4 + hf
                    em.wge(v, elA, gi)
                    ins = v.tensor_scalar(tmpAB[:, :], qrp4[hpp][hpar][:, hf * 1024:(hf + 1) * 1024],
                                          0.0, None, ALU.min)
                    ins.then_inc(elV, 1)
                    em.wge(v, elA, gi + 1)
                    ins = v.scalar_tensor_tensor(q_feat[:, hf * 1024:(hf + 1) * 1024],
                                                 qrp4[hpp][hpar][:, hf * 1024:(hf + 1) * 1024],
                                                 0.0, tmpAB[:, :], ALU.max, ALU.add)
                    ins.then_inc(qfR, 1)
                    em.wge(v, qfR, 2 * h + hf + 1)

            em.wge(v, mmV, 64)
            for tc in range(4):
                q_rope_tc(0, tc, 0)
            for g2 in range(4):
                kelu(0, g2)
            q_elu(0)

            for h_idx in range(NH):
                hh = HORDER[h_idx]
                par = h_idx % 2
                pp = (h_idx // 2) % 2
                em.wge(v, mmM, h_idx + 1)
                ins = v.tensor_copy(memSB[:, 0:130], psMEM)
                ins.then_inc(evM, 1)
                em.wge(v, evM, h_idx + 1)
                for w in range(2):
                    em.wge(v, mmMO, h_idx * 2 + w + 1)
                    for j in range(8):
                        qsub = w * 8 + j
                        ins = v.tensor_copy(memout[:, qsub * 130:(qsub + 1) * 130],
                                            psMO[j][:, 0:130])
                    ins.then_inc(evMO, 1)
                    em.wge(v, evMO, h_idx * 2 + w + 1)
                denl = memout.ap().rearrange("p (q c) -> p q c", c=130)[:, :, 128]
                vf(v.reciprocal(stat[:, 0:16], denl))
                vf(v.tensor_scalar(stat[:, 0:16], stat[:, 0:16], gcol[:, hh:hh + 1], None, ALU.mult))
                for qsub in range(16):
                    vf(v.tensor_scalar(tmpP16[:, qsub * 128:(qsub + 1) * 128],
                                       memout[:, qsub * 130:qsub * 130 + 128],
                                       stat[:, qsub:qsub + 1], None, ALU.mult))
                for qc in range(4):
                    if h_idx % 2 == 1 and h_idx < NH - 1:
                        q_rope_tc((h_idx + 1) // 2, qc, h_idx * 6 + 6 + qc)
                    if h_idx < NH - 1:
                        kelu(h_idx + 1, qc)
                        if qc == 3:
                            q_elu(h_idx + 1)
                    em.wge(v, pvK, (h_idx * 4 + qc) * 32 + 32)
                    if h_idx * 4 + qc >= 2:
                        em.wge(v, cmf, h_idx * 4 + qc - 1)
                    for j in range(4):
                        ins = v.tensor_copy(den_sb[:, j:j + 1], psPV[j][:, 128:129])
                    vf(ins)
                    vf(v.reciprocal(den_sb[:, 0:4], den_sb[:, 0:4]))
                    vf(v.tensor_scalar(den_sb[:, 0:4], den_sb[:, 0:4], gicol[:, hh:hh + 1], None, ALU.mult))
                    for j in range(4):
                        qsub = qc * 4 + j
                        vf(v.scalar_tensor_tensor(cmbTM[qc % 2][:, j * 128:(j + 1) * 128],
                                                  psPV[j][:, 0:128], den_sb[:, j:j + 1],
                                                  tmpP16[:, qsub * 128:(qsub + 1) * 128],
                                                  ALU.mult, ALU.add))
                    v.engine_nop().then_inc(cmb, 1)
                for g2 in range(2):
                    em.wge(v, cmf, h_idx * 4 + (g2 + 1) * 2)
                    if h_idx > 0:
                        em.wge(v, cmW[g2], 16 * h_idx)
                    ins = v.tensor_copy(cmbOut[g2][:, :], psLN[:, :])
                    ins.then_inc(cmE, 1)
                    em.wge(v, cmE, h_idx * 2 + g2 + 1)

            for it in range(NTIL):
                for ic in range(4):
                    em.wge(v, woF, (it * 4 + ic + 1) * DC)
                    for tsub in range(4):
                        if ic == 0:
                            em.wge(v, sXt[tsub], 16 * ((xev("wo", it, 0) // 4) + 1))
                        em.wge(v, oS[tsub % 2],
                               16 * (OS_B["wo"] + (it * 4 + ic) * 2 + tsub // 2))
                        v.tensor_tensor(outst[tsub % 2][:, :], psB[tsub],
                                        xt[tsub][:, ic * 512:(ic + 1) * 512],
                                        ALU.add).then_inc(evB, 1)

            if stage == "attn":
                return
            ffn_vector("ffn2")

    return nc


def _prep_core_inputs(inputs, core):
    b, half = core // 2, core % 2
    x = np.asarray(inputs["L0_attn_hidden_states"])
    xs = np.ascontiguousarray(x[b, half * T:(half + 1) * T, :], dtype=np.float32)

    s0 = half * T
    t = np.arange(s0, s0 + T, dtype=np.float32)
    inv_freq = (1.0 / (10000.0 ** (np.arange(0, D, 2, dtype=np.float32) / np.float32(D)))).astype(np.float32)
    freqs = (inv_freq[:, None] * t[None, :]).astype(np.float32)
    cos = np.cos(freqs).astype(np.float32)
    sin = np.sin(freqs).astype(np.float32)

    bup = np.ascontiguousarray(np.asarray(inputs["L0_ffn_b_up"], np.float32).reshape(HC, 128).T)
    bdn = np.ascontiguousarray(np.asarray(inputs["L0_ffn_b_down"], np.float32).reshape(1, D))
    gate = np.ascontiguousarray(np.asarray(inputs["L0_attn_gate"], np.float32).reshape(1, NH))

    return dict(
        x_in=xs,
        wq_in=np.asarray(inputs["L0_attn_Wq"], np.float32),
        wk_in=np.asarray(inputs["L0_attn_Wk"], np.float32),
        wv_in=np.asarray(inputs["L0_attn_Wv"], np.float32),
        wo_in=np.asarray(inputs["L0_attn_Wo"], np.float32),
        gate_in=gate,
        wupq_in=np.asarray(inputs["L0_ffn_w_up_q"], np.int32),
        wups_in=np.asarray(inputs["L0_ffn_w_up_scale"], np.float32),
        bup_in=bup,
        ua_in=np.ascontiguousarray(np.asarray(inputs["L0_ffn_w_up_lora_a"], np.float32)
                                   .reshape(DC, 128, R).transpose(1, 0, 2).reshape(128, DC * R)),
        ub_in=np.asarray(inputs["L0_ffn_w_up_lora_b"], np.float32),
        wdnq_in=np.asarray(inputs["L0_ffn_w_down_q"], np.int32),
        wdns_in=np.asarray(inputs["L0_ffn_w_down_scale"], np.float32),
        bdn_in=bdn,
        da_in=np.ascontiguousarray(np.asarray(inputs["L0_ffn_w_down_lora_a"], np.float32)
                                   .reshape(HC, 128, R).transpose(1, 0, 2).reshape(128, HC * R)),
        db_in=np.asarray(inputs["L0_ffn_w_down_lora_b"], np.float32),
        cos_in=cos,
        sin_in=sin,
        ident_in=np.eye(128, dtype=ml_dtypes.bfloat16),
    )


def kernel(**inputs):
    in_maps = [_prep_core_inputs(inputs, c) for c in range(8)]
    nc = build(_STAGE)
    res = run_bass_kernel_spmd(nc, in_maps, core_ids=list(range(8)))
    out = np.zeros((4, SEQ, D), dtype=np.float32)
    for c in range(8):
        b, half = c // 2, c % 2
        out[b, half * T:(half + 1) * T, :] = np.asarray(res.results[c]["out_d"])
    return out


# version 11
# speedup vs baseline: 1.0499x; 1.0499x over previous
import numpy as np
import ml_dtypes

import concourse.bass as bass
import concourse.mybir as mybir
from concourse.bass_utils import run_bass_kernel_spmd

F32 = mybir.dt.float32
BF16 = mybir.dt.bfloat16
I32 = mybir.dt.int32
AF = mybir.ActivationFunctionType
ALU = mybir.AluOpType
AX = mybir.AxisListType

T = 2048
D = 2048
H = 8192
R = 16
NH = 16
HD = 128
SEQ = 4096
DC = D // 128
HC = H // 128
TT = 512
NTIL = T // TT

_STAGE = "full"

INV_D = 1.0 / D
EPS = 1e-5
EXP_BIAS = -20.0
INV_SQRT_HD = float(1.0 / np.sqrt(128.0))
HORDER = [0, 8, 1, 9, 2, 10, 3, 11, 4, 12, 5, 13, 6, 14, 7, 15]
PAIRS = [[0, 1], [2, 3], [4, 5], [6, 7]]


class Emitter:
    def wge(self, eng, sem, v):
        if v <= 0:
            return
        eng.wait_ge(sem, v)


em = Emitter()


def sidx(h_idx, qc, kt):
    return (h_idx * 4 + qc) * 32 + kt


def build(stage):
    nc = bass.Bass("TRN2", target_bir_lowering=False)

    ap = lambda t: t.ap()
    x_in = ap(nc.dram_tensor("x_in", [T, D], F32, kind="ExternalInput"))
    wq_in = ap(nc.dram_tensor("wq_in", [D, D], F32, kind="ExternalInput"))
    wk_in = ap(nc.dram_tensor("wk_in", [D, D], F32, kind="ExternalInput"))
    wv_in = ap(nc.dram_tensor("wv_in", [D, D], F32, kind="ExternalInput"))
    wo_in = ap(nc.dram_tensor("wo_in", [D, D], F32, kind="ExternalInput"))
    gate_in = ap(nc.dram_tensor("gate_in", [1, NH], F32, kind="ExternalInput"))
    wupq_in = ap(nc.dram_tensor("wupq_in", [H, D], I32, kind="ExternalInput"))
    wups_in = ap(nc.dram_tensor("wups_in", [H, 32], F32, kind="ExternalInput"))
    bup_in = ap(nc.dram_tensor("bup_in", [128, HC], F32, kind="ExternalInput"))
    ua_in = ap(nc.dram_tensor("ua_in", [128, DC * R], F32, kind="ExternalInput"))
    ub_in = ap(nc.dram_tensor("ub_in", [R, H], F32, kind="ExternalInput"))
    wdnq_in = ap(nc.dram_tensor("wdnq_in", [D, H], I32, kind="ExternalInput"))
    wdns_in = ap(nc.dram_tensor("wdns_in", [D, 128], F32, kind="ExternalInput"))
    bdn_in = ap(nc.dram_tensor("bdn_in", [1, D], F32, kind="ExternalInput"))
    da_in = ap(nc.dram_tensor("da_in", [128, HC * R], F32, kind="ExternalInput"))
    db_in = ap(nc.dram_tensor("db_in", [R, D], F32, kind="ExternalInput"))
    cos_in = ap(nc.dram_tensor("cos_in", [1024, T], F32, kind="ExternalInput"))
    sin_in = ap(nc.dram_tensor("sin_in", [1024, T], F32, kind="ExternalInput"))
    ident_in = ap(nc.dram_tensor("ident_in", [128, 128], BF16, kind="ExternalInput"))

    out_d = ap(nc.dram_tensor("out_d", [T, D], F32, kind="ExternalOutput"))

    resid1 = ap(nc.dram_tensor("resid1", [T, D], F32))
    resid2 = ap(nc.dram_tensor("resid2", [T, D], F32))
    wupT_c = ap(nc.dram_tensor("wupT_c", [HC * 128, D], BF16))
    wdnT_c = ap(nc.dram_tensor("wdnT_c", [HC * 128, D], BF16))
    wq_bf = ap(nc.dram_tensor("wq_bf", [D, D], BF16))
    wk_bf = ap(nc.dram_tensor("wk_bf", [D, D], BF16))
    wv_bf = ap(nc.dram_tensor("wv_bf", [D, D], BF16))
    wo_bf = ap(nc.dram_tensor("wo_bf", [D, D], BF16))
    wqT_c = ap(nc.dram_tensor("wqT_c", [DC * 128, D], BF16))
    wkT_c = ap(nc.dram_tensor("wkT_c", [DC * 128, D], BF16))
    wvT_c = ap(nc.dram_tensor("wvT_c", [DC * 128, D], BF16))
    woT_c = ap(nc.dram_tensor("woT_c", [DC * 128, D], BF16))
    cmb_d = ap(nc.dram_tensor("cmb_d", [D, T], BF16))
    sh_k = ap(nc.dram_tensor("sh_k", [2 * NH * 128, T], BF16, addr_space="Shared"))
    sh_v = ap(nc.dram_tensor("sh_v", [2 * NH * T, 128], BF16, addr_space="Shared"))
    kstg_k = ap(nc.dram_tensor("kstg_k", [NH * 128, T], BF16))
    kstg_v = ap(nc.dram_tensor("kstg_v", [NH * T, 128], BF16))
    bar_i = ap(nc.dram_tensor("bar_i", [1, 4], F32))
    bar_o = ap(nc.dram_tensor("bar_o", [1, 4], F32))

    from contextlib import ExitStack
    with ExitStack() as ctx:
        e = ctx.enter_context
        offs = {}

        def sbuf(name, shape, dtype):
            t = e(nc.sbuf_tensor(name, shape, dtype))
            sz = shape[1] * ({F32: 4, BF16: 2, I32: 4}[dtype])
            offs[name] = nc.sbuf_base - sz
            return t

        def overlay(name, shape, dtype, at, plus=0):
            return nc.alloc_sbuf_tensor_at(name, shape, dtype, offset=offs[at] + plus)

        ident = sbuf("ident", [128, 128], BF16)
        bupc = sbuf("bupc", [128, HC], F32)
        bdnr = sbuf("bdnr", [1, D], BF16)
        ones1 = sbuf("ones1", [1, 128], BF16)
        loraA = sbuf("loraA", [128, DC * R], BF16)
        loraB = sbuf("loraB", [R, H], BF16)
        dnA = sbuf("dnA", [128, HC * R], BF16)
        dnB = sbuf("dnB", [R, D], BF16)
        grow = sbuf("grow", [1, NH], F32)
        growb = sbuf("growb", [1, NH], BF16)
        gcol = sbuf("gcol", [128, NH], F32)
        gicol = sbuf("gicol", [128, NH], F32)
        cstc = sbuf("cstc", [128, 4], F32)
        zrow = sbuf("zrow", [1, 4], F32)
        zb512 = sbuf("zb512", [1, 512], BF16)
        memSB = sbuf("memSB", [128, 130], BF16)
        den_sb = sbuf("den_sb", [128, 8], F32)

        xt = [sbuf(f"xt{i}", [128, D], F32) for i in range(4)]
        sq = sbuf("sq", [128, D], F32)
        lntm = [sbuf(f"lntm{i}", [128, D], BF16) for i in range(2)]
        stat = sbuf("stat", [128, 16], F32)
        ln1fm = sbuf("ln1fm", [128, DC * TT], BF16)
        x2fm = sbuf("x2fm", [128, HC * TT], BF16)
        outst = [sbuf(f"outst{i}", [128, 512], F32) for i in range(2)]
        xa_sb = sbuf("xa_sb", [R, TT], BF16)
        xda_sb = sbuf("xda_sb", [R, TT], BF16)
        wbufA = [sbuf(f"wbufA{i}", [128, D], BF16) for i in range(2)]
        wbufB = [sbuf(f"wbufB{i}", [128, 512], BF16) for i in range(4)]
        bufQ = [sbuf(f"bufQ{i}", [128, D], F32) for i in range(2)]
        bufD = [sbuf(f"bufD{i}", [128, D], BF16) for i in range(2)]
        bufT = [sbuf(f"bufT{i}", [128, D], BF16) for i in range(2)]
        bufS = [sbuf(f"bufS{i}", [128, 32], F32) for i in range(2)]

        qrp4 = [[overlay("qrp00", [128, T], BF16, "lntm0"),
                 overlay("qrp01", [128, T], BF16, "lntm1")],
                [overlay("qrp10", [128, T], BF16, "xt1"),
                 overlay("qrp11", [128, T], BF16, "xt1", plus=4096)]]

        klo_sb = overlay("klo_sb", [128, 512], F32, "sq")
        tmpAB = overlay("tmpAB", [128, 1024], F32, "sq", plus=2048)
        tmpP = overlay("tmpP", [128, 128], BF16, "sq", plus=6144)
        k_buf = [overlay("k_buf0", [128, SEQ], BF16, "bufQ0"),
                 overlay("k_buf1", [128, SEQ], BF16, "bufQ1")]
        awbuf = [overlay("awbuf0", [128, D], BF16, "bufQ0"),
                 overlay("awbuf1", [128, D], BF16, "bufQ1")]
        v_buf = [overlay("v_buf0", [128, 32 * 130], BF16, "bufD0"),
                 overlay("v_buf1", [128, 32 * 130], BF16, "bufD0", plus=8320)]
        kf_tm = overlay("kf_tm", [128, SEQ], BF16, "ln1fm")
        ptb = [overlay("ptb0", [128, 1024], BF16, "ln1fm", plus=8192),
               overlay("ptb1", [128, 1024], BF16, "ln1fm", plus=10240)]
        q_feat = overlay("q_feat", [128, T], BF16, "ln1fm", plus=12288)
        memout = overlay("memout", [128, 16 * 130], BF16, "outst0")
        kst = [overlay("kst0", [128, 512], BF16, "wbufB0", plus=2048),
               overlay("kst1", [128, 512], BF16, "wbufB0", plus=3072)]
        cmbTM = [overlay("cmbTM0", [128, 512], BF16, "wbufB0"),
                 overlay("cmbTM1", [128, 512], BF16, "wbufB0", plus=1024)]
        cs2c = [overlay("cs2c0", [128, 512], F32, "xt0"),
                overlay("cs2c1", [128, 512], F32, "xt0", plus=2048)]
        cs2s = [overlay("cs2s0", [128, 512], F32, "xt0", plus=4096),
                overlay("cs2s1", [128, 512], F32, "xt0", plus=6144)]
        tmpP16 = overlay("tmpP16", [128, 2048], BF16, "xt2")
        cmbOut = [overlay("cmbOut0", [128, 1024], BF16, "xt3"),
                  overlay("cmbOut1", [128, 1024], BF16, "xt3", plus=2048)]
        wvbuf = overlay("wvbuf", [128, 16 * 512], BF16, "ln1fm")
        vst = [overlay("vst0", [128, 512], BF16, "wbufA0"),
               overlay("vst1", [128, 512], BF16, "wbufA1")]
        woCbuf = overlay("woCbuf", [128, 16 * 512], BF16, "ln1fm")

        PS = [e(nc.psum_tensor(f"PS{i}", [128, 1024], F32)) for i in range(4)]
        psA = [PS[0][:, 0:512], PS[0][:, 512:1024]]
        psB = [PS[1][:, 0:512], PS[1][:, 512:1024], PS[2][:, 0:512], PS[2][:, 512:1024]]
        psXA = PS[1][0:R, 0:512]
        psDQ = PS[3].ap().bitcast(BF16)[:, 0:1024]
        psDQ3 = psDQ.rearrange("p (a c) -> p a c", c=128)
        psLN = PS[3].ap().bitcast(BF16)[:, 1024:2048]
        psLN3 = psLN.rearrange("p (a c) -> p a c", c=128)
        psPJ = PS[3][:, 0:512]
        psMEM = PS[3][:, 0:130]
        psS = [PS[0][:, 0:512], PS[0][:, 512:1024]]
        psPV = [PS[1][:, 0:130], PS[1][:, 512:642], PS[2][:, 0:130], PS[2][:, 512:642]]
        psMO = [PS[1][:, 0:130], PS[1][:, 130:260], PS[1][:, 260:390],
                PS[1][:, 512:642], PS[1][:, 642:772], PS[1][:, 772:902],
                PS[2][:, 0:130], PS[2][:, 130:260]]
        psV = [PS[0][:, 0:512], PS[0][:, 512:1024]]
        psG = PS[3][:, 512:528]

        def sem(name):
            return e(nc.semaphore(name))

        sXt = [sem(f"sXt{i}") for i in range(4)]; sC = sem("sC")
        dqL = [sem("dqL0"), sem("dqL1")]
        dqV = sem("dqV"); dqT = sem("dqT"); dqE = sem("dqE"); dqW = sem("dqW")
        wAL = [sem("wAL0"), sem("wAL1")]; wAF = sem("wAF")
        wBL = [sem(f"wBL{i}") for i in range(4)]; wBF = sem("wBF")
        lnQ = sem("lnQ"); lnR = sem("lnR"); lnV = sem("lnV"); lnST = sem("lnST")
        lnN = sem("lnN"); lnT = sem("lnT"); lnE = sem("lnE")
        mmXA = sem("mmXA"); eXA = sem("eXA"); mmXDA = sem("mmXDA"); eXDA = sem("eXDA")
        mmA = sem("mmA"); evA = sem("evA"); evB = sem("evB")
        oS = [sem("oS0"), sem("oS1")]
        vsy = sem("vsy"); ssy = sem("ssy"); ini = sem("ini")
        awc = sem("awc"); awt = [sem("awt0"), sem("awt1")]; aww = sem("aww")
        mmP = sem("mmP"); evP = sem("evP")
        kwL = [sem("kwL0"), sem("kwL1")]; kwF = sem("kwF")
        csL = [sem("csL0"), sem("csL1")]; csF = sem("csF")
        kwW = [sem("kwW0"), sem("kwW1")]
        barS = sem("barS"); barC = sem("barC")
        vwL = sem("vwL"); vwF = sem("vwF")
        mmV = sem("mmV"); evV = sem("evV"); vstW = [sem("vstW0"), sem("vstW1")]
        kL = [sem("kL0"), sem("kL1")]
        vL = [sem("vL0"), sem("vL1")]
        kfT = sem("kfT"); kfE = sem("kfE")
        elV = sem("elV"); elA = sem("elA"); qfR = sem("qfR")
        mmM = sem("mmM"); evM = sem("evM"); mmMO = sem("mmMO"); evMO = sem("evMO")
        mmS = sem("mmS"); exS = sem("exS"); pvK = sem("pvK")
        cmb = sem("cmb"); cmf = sem("cmf"); cmE = sem("cmE")
        cmW = [sem("cmW0"), sem("cmW1")]
        woCL = [sem("woCL0"), sem("woCL1")]; woCF = sem("woCF"); woCH = sem("woCH")
        woL = [sem(f"woL{i}") for i in range(4)]; woF = sem("woF")
        gsb = sem("gsb"); mmG = sem("mmG"); evG = sem("evG"); vOn = sem("vOn")

        def lt_of(phase, it):
            return {"ffn1": 0, "attn": 4, "ffn2": 8}[phase] + it

        def xev(phase, it, tsub):
            base = {"ffn1": 0, "ln2": 16, "wo": 32, "ffn2": 48}[phase]
            return base + it * 4 + tsub

        def evb_ffn(tile):
            return tile * 16 + (64 if tile >= NTIL else 0)

        OS_B = {"ffn1": 0, "wo": 32, "ffn2": 64}

        ln1fm4 = ln1fm.ap().rearrange("p (dc s c) -> p dc s c", s=4, c=128)
        ln2fm = x2fm
        ln2fm4 = x2fm.ap().rearrange("p (dc tc c) -> p dc tc c", tc=16, c=128)

        do_attn = stage != "ffn1"
        do_ffn2 = stage == "full"

        block = e(nc.Block())

        @block.gpsimd
        def _(g):
            g.dma_start(out=ident[:, :], in_=ident_in).then_inc(sC, 16)
            g.dma_start(out=bupc[:, :], in_=bup_in).then_inc(sC, 16)
            g.dma_start(out=bdnr[:, :], in_=bdn_in).then_inc(sC, 16)
            g.dma_start(out=loraA[:, :], in_=ua_in).then_inc(sC, 16)
            g.dma_start(out=loraB[:, :], in_=ub_in).then_inc(sC, 16)
            g.dma_start(out=dnA[:, :], in_=da_in).then_inc(sC, 16)
            g.dma_start(out=dnB[:, :], in_=db_in).then_inc(sC, 16)
            g.dma_start(out=grow[:, :], in_=gate_in).then_inc(sC, 16)

            def cache_write(m):
                em.wge(g, dqW, 16 * m)
                if m < HC:
                    em.wge(g, dqE, 2 * (m + 1))
                    g.dma_start(out=wupT_c[m * 128:(m + 1) * 128, :],
                                in_=bufT[m % 2][:, :]).then_inc(dqW, 16)
                else:
                    km = m - HC
                    dcm, hqm = km % 16, km // 16
                    em.wge(g, dqE, 2 * (m + 1))
                    dst3 = wdnT_c[hqm * 2048:(hqm + 1) * 2048, dcm * 128:(dcm + 1) * 128]
                    dst3 = dst3.rearrange("(j r) c -> r j c", r=128)
                    src3 = bufT[m % 2].ap().rearrange("r (j c) -> r j c", c=128)
                    g.dma_start(out=dst3, in_=src3).then_inc(dqW, 16)

            for n in range(2 * HC):
                par = n % 2
                em.wge(g, dqV, n - 1)
                if n < HC:
                    srcq = wupq_in[n * 128:(n + 1) * 128, :]
                    srcs = wups_in[n * 128:(n + 1) * 128, :]
                else:
                    k = n - HC
                    dc, hq = k % 16, k // 16
                    srcq = wdnq_in[dc * 128:(dc + 1) * 128, hq * 2048:(hq + 1) * 2048]
                    srcs = wdns_in[dc * 128:(dc + 1) * 128, hq * 32:(hq + 1) * 32]
                for qt in range(4):
                    g.dma_start(out=bufQ[par][qt * 32:(qt + 1) * 32, :],
                                in_=srcq[qt * 32:(qt + 1) * 32, :]).then_inc(dqL[par], 16)
                g.dma_start(out=bufS[par][:, :], in_=srcs).then_inc(dqL[par], 16)
                if n >= 1:
                    cache_write(n - 1)
            cache_write(2 * HC - 1)

            if do_attn:
                for mi, (w_in_, w_bf_) in enumerate(((wq_in, wq_bf), (wk_in, wk_bf),
                                                     (wv_in, wv_bf), (wo_in, wo_bf))):
                    em.wge(g, evB, 16 + 4 * mi)
                    em.wge(g, awc, 16 * mi)
                    g.dma_start(out=w_bf_, in_=w_in_).then_inc(awc, 16)
                gpid = nc.partition_id(engines=[mybir.EngineType.Pool])
                gslot = nc.snap(gpid % 2, engines=[mybir.EngineType.Pool])
                shk3g = sh_k.rearrange("(s r) t -> s r t", s=2)
                shv3g = sh_v.rearrange("(s r) c -> s r c", s=2)
                em.wge(g, ini, 1)
                g.dma_start(out=bar_i, in_=zrow[:, :]).then_inc(barS, 16)
                em.wge(g, kwW[0], 16 * 32)
                em.wge(g, kwW[1], 16 * 32)
                g.dma_start(out=shk3g[bass.ts(gslot, 1), :, :],
                            in_=kstg_k.unsqueeze(0)).then_inc(barS, 16)
                em.wge(g, vstW[0], 16 * 128)
                em.wge(g, vstW[1], 16 * 128)
                g.dma_start(out=shv3g[bass.ts(gslot, 1), :, :],
                            in_=kstg_v.unsqueeze(0)).then_inc(barS, 16)
                em.wge(g, barS, 48)
                g.collective_compute("AllReduce", ALU.add, replica_groups=PAIRS,
                                     ins=[bar_i.opt()], outs=[bar_o.opt()]).then_inc(barC, 1)

        @block.sync
        def _(s):
            pid = nc.partition_id(engines=[mybir.EngineType.SP])
            myslot = nc.snap(pid % 2, engines=[mybir.EngineType.SP])
            sh_k3 = sh_k.rearrange("(s r) t -> s r t", s=2)
            sh_v3 = sh_v.rearrange("(s r) c -> s r c", s=2)

            aw_list = ((wq_bf, wqT_c), (wk_bf, wkT_c), (wv_bf, wvT_c), (wo_bf, woT_c)) \
                if do_attn else ()
            aw_state = {"n": 0}

            def aw_step():
                st_ = aw_state["n"]
                if st_ > 64:
                    return
                if st_ < 64:
                    mi, jc = st_ // 16, st_ % 16
                    em.wge(s, awc, 16 * (mi + 1))
                    if st_ < 2:
                        em.wge(s, dqV, 2 * HC)
                    else:
                        em.wge(s, aww, 16 * (st_ - 1))
                    s.dma_start(out=awbuf[st_ % 2][:, :],
                                in_=aw_list[mi][0][:, jc * 128:(jc + 1) * 128],
                                transpose=True).then_inc(awt[st_ % 2], 16)
                if st_ >= 1:
                    pm = st_ - 1
                    mi, jc = pm // 16, pm % 16
                    em.wge(s, awt[pm % 2], 16 * (pm // 2 + 1))
                    s.dma_start(out=aw_list[mi][1][jc * 128:(jc + 1) * 128, :],
                                in_=awbuf[pm % 2][:, :]).then_inc(aww, 16)
                aw_state["n"] += 1

            def ffn_sync(phase, src, dst):
                p = {"ffn1": 0, "ffn2": 1}[phase]
                for it in range(NTIL):
                    tile = p * NTIL + it
                    for tsub in range(4):
                        if it > 0 or (phase == "ffn1" and tile > 0):
                            em.wge(s, evB, evb_ffn(tile - 1) + 12 + tsub + 1)
                        if phase == "ffn2" and it == 0:
                            em.wge(s, evB, 64 + 64)
                            em.wge(s, evP, 128)
                            em.wge(s, oS[0], 16 * 64)
                            em.wge(s, oS[1], 16 * 64)
                        r0 = it * TT + tsub * 128
                        s.dma_start(out=xt[tsub][:, :], in_=src[r0:r0 + 128, :]).then_inc(sXt[tsub], 16)
                    for hc in range(HC):
                        ev = tile * HC + hc
                        if do_attn and phase == "ffn1" and tile >= 2 and \
                                (hc % 2 == 0 or (tile == 3 and hc == 61)):
                            aw_step()
                        if tile == 0:
                            em.wge(s, dqW, 16 * (hc + 1))
                        else:
                            em.wge(s, dqW, 16 * HC)
                        if phase == "ffn2" and it == 0 and hc < 2:
                            em.wge(s, aww, 16 * 64)
                            em.wge(s, kwF, 32)
                        em.wge(s, wAF, ev - 1)
                        s.dma_start(out=wbufA[hc % 2][:, :],
                                    in_=wupT_c[hc * 128:(hc + 1) * 128, :]).then_inc(wAL[hc % 2], 16)
                    for dch in range(4):
                        for tsub in range(4):
                            em.wge(s, evB, evb_ffn(tile) + dch * 4 + tsub + 1)
                            r0 = it * TT + tsub * 128
                            s.dma_start(out=dst[r0:r0 + 128, dch * 512:(dch + 1) * 512],
                                        in_=outst[tsub % 2][:, :]).then_inc(oS[tsub % 2], 16)

            ffn1_dst = out_d if stage == "ffn1" else resid1
            ffn_sync("ffn1", x_in, ffn1_dst)
            if stage == "ffn1":
                em.wge(s, oS[0], 16 * 8 * NTIL)
                em.wge(s, oS[1], 16 * 8 * NTIL)
                return

            for it in range(NTIL):
                for tsub in range(4):
                    em.wge(s, oS[0], 128 * (it + 1))
                    em.wge(s, oS[1], 128 * (it + 1))
                    if it == 0:
                        em.wge(s, evB, 48 + 12 + tsub + 1)
                    else:
                        em.wge(s, lnN, 4 * (4 + it - 1) + 4)
                    r0 = it * TT + tsub * 128
                    s.dma_start(out=xt[tsub][:, :], in_=resid1[r0:r0 + 128, :]).then_inc(sXt[tsub], 16)

            def w_head_load(kl, cache, hh):
                em.wge(s, kwF, kl - 1)
                if kl < 2:
                    em.wge(s, aww, 16 * 64)
                for qtr in range(4):
                    src3 = cache[:, hh * 128:(hh + 1) * 128]
                    src3 = src3.rearrange("(jc p) c -> p jc c", p=128)[:, qtr * 4:(qtr + 1) * 4, :]
                    dst3 = wbufA[kl % 2].ap().rearrange("p (jc c) -> p jc c", c=128)[
                        :, qtr * 4:(qtr + 1) * 4, :]
                    s.dma_start(out=dst3, in_=src3).then_inc(kwL[kl % 2], 16)

            for pc in range(8):
                for hi in range(2):
                    kl = pc * 2 + hi
                    w_head_load(kl, wkT_c, pc + 8 * hi)
                for tc in range(4):
                    u = pc * 4 + tc
                    for hi in range(2):
                        hh = pc + 8 * hi
                        em.wge(s, evP, 2 * u + 2)
                        s.dma_start(out=kstg_k[hh * 128:(hh + 1) * 128, tc * 512:(tc + 1) * 512],
                                    in_=kst[hi][:, :]).then_inc(kwW[hi], 16)

            for hg in range(4):
                em.wge(s, vwF, hg)
                for qtr in range(4):
                    src3 = wvT_c[:, hg * 512:(hg + 1) * 512]
                    src3 = src3.rearrange("(jc p) c -> p jc c", p=128)[:, qtr * 4:(qtr + 1) * 4, :]
                    dst3 = wvbuf.ap().rearrange("p (jc c) -> p jc c", c=512)[:, qtr * 4:(qtr + 1) * 4, :]
                    s.dma_start(out=dst3, in_=src3).then_inc(vwL, 16)
                for tsub in range(16):
                    vidx = hg * 16 + tsub
                    em.wge(s, evV, vidx + 1)
                    for j in range(4):
                        hh = hg * 4 + j
                        s.dma_start(out=kstg_v[hh * T + tsub * 128: hh * T + (tsub + 1) * 128, :],
                                    in_=vst[vidx % 2][:, j * 128:(j + 1) * 128]
                                    ).then_inc(vstW[vidx % 2], 16)

            def cmb_store(h):
                hhp = HORDER[h]
                for g2 in range(2):
                    em.wge(s, cmE, h * 2 + g2 + 1)
                    s.dma_start(out=cmb_d[hhp * 128:(hhp + 1) * 128, g2 * 1024:(g2 + 1) * 1024],
                                in_=cmbOut[g2][:, :]).then_inc(cmW[g2], 16)

            for h_idx in range(NH):
                hh = HORDER[h_idx]
                par = h_idx % 2
                em.wge(s, barC, 1)
                if h_idx < 2:
                    em.wge(s, aww, 16 * 64)
                em.wge(s, mmS, (h_idx - 1) * 128)
                for slot in range(2):
                    s.dma_start(out=k_buf[par][:, slot * 2048:(slot + 1) * 2048],
                                in_=sh_k[(slot * 16 + hh) * 128:(slot * 16 + hh) * 128 + 128, :]
                                ).then_inc(kL[par], 16)
                em.wge(s, pvK, (h_idx - 1) * 128)
                for slot in range(2):
                    for ktg in range(4):
                        rb = (slot * 16 + hh) * T + ktg * 512
                        src3 = sh_v[rb:rb + 512, :].rearrange("(kt t) c -> t kt c", t=128)
                        base_kt = slot * 16 + ktg * 4
                        dst3 = v_buf[par].ap().rearrange("p (kt c) -> p kt c", c=130)[
                            :, base_kt:base_kt + 4, 0:128]
                        s.dma_start(out=dst3, in_=src3).then_inc(vL[par], 16)
                if h_idx == 0:
                    for hi in range(2):
                        w_head_load(16 + hi, wqT_c, HORDER[hi])
                if h_idx % 2 == 1 and h_idx < NH - 1:
                    for hi in range(2):
                        kl = 17 + h_idx + hi
                        w_head_load(kl, wqT_c, HORDER[h_idx + 1 + hi])
                if h_idx > 0:
                    cmb_store(h_idx - 1)
            cmb_store(NH - 1)

            wo_dst = out_d if stage == "attn" else resid2
            for it in range(NTIL):
                for jc in range(DC):
                    if jc == 0:
                        em.wge(s, woCH, max(0, 2 * it - 1))
                        if it == 0:
                            em.wge(s, cmW[0], 16 * NH)
                            em.wge(s, cmW[1], 16 * NH)
                    elif jc == 8:
                        em.wge(s, woCH, 2 * it)
                    s.dma_start(out=woCbuf[:, jc * 512:(jc + 1) * 512],
                                in_=cmb_d[jc * 128:(jc + 1) * 128, it * 512:(it + 1) * 512]
                                ).then_inc(woCL[jc // 8], 16)
                for tsub in range(4):
                    if it > 0:
                        em.wge(s, evB, 64 + (it - 1) * 16 + 12 + tsub + 1)
                    else:
                        em.wge(s, evP, 128)
                        em.wge(s, oS[0], 16 * 32)
                        em.wge(s, oS[1], 16 * 32)
                    r0 = it * TT + tsub * 128
                    s.dma_start(out=xt[tsub][:, :], in_=resid1[r0:r0 + 128, :]).then_inc(sXt[tsub], 16)
                def wo_stores(ic):
                    for tsub in range(4):
                        em.wge(s, evB, 64 + (it * 4 + ic) * 4 + tsub + 1)
                        r0 = it * TT + tsub * 128
                        s.dma_start(out=wo_dst[r0:r0 + 128, ic * 512:(ic + 1) * 512],
                                    in_=outst[tsub % 2][:, :]).then_inc(oS[tsub % 2], 16)

                for ic in range(4):
                    if ic > 0:
                        wo_stores(ic - 1)
                    for jc in range(DC):
                        wev = (it * 4 + ic) * DC + jc
                        em.wge(s, woF, wev - 3)
                        s.dma_start(out=wbufB[jc % 4][:, :],
                                    in_=woT_c[jc * 128:(jc + 1) * 128, ic * 512:(ic + 1) * 512]
                                    ).then_inc(woL[jc % 4], 16)
                wo_stores(3)

            if stage == "attn":
                em.wge(s, oS[0], 16 * (OS_B["wo"] + 32))
                em.wge(s, oS[1], 16 * (OS_B["wo"] + 32))
                return

            ffn_sync("ffn2", resid2, out_d)
            em.wge(s, oS[0], 16 * (OS_B["ffn2"] + 32))
            em.wge(s, oS[1], 16 * (OS_B["ffn2"] + 32))

        @block.scalar
        def _(sc):
            sctr = {"n": 0}

            def sf(ins):
                sctr["n"] += 1
                ins.then_inc(ssy, 1)
                em.wge(sc, ssy, sctr["n"])

            em.wge(sc, sC, 128)
            em.wge(sc, ini, 1)
            sf(sc.activation(grow[0:1, :], grow[0:1, :], AF.Sigmoid))

            def ln_scalar(phase, it):
                lt = lt_of(phase, it)
                xph = {"ffn1": "ffn1", "ffn2": "ffn2", "attn": "ln2"}[phase]
                for tsub in range(4):
                    em.wge(sc, sXt[tsub], 16 * ((xev(xph, it, 0) // 4) + 1))
                    em.wge(sc, lnR, 8 * lt + 2 * tsub)
                    sc.activation(sq[:, :], xt[tsub][:, :], AF.Square).then_inc(lnQ, 1)
                em.wge(sc, lnV, lt + 1)
                sf(sc.activation(stat[:, 8:12], stat[:, 4:8], AF.Ln, bias=cstc[:, 0:1]))
                sc.activation(stat[:, 12:16], stat[:, 8:12], AF.Exp, scale=-0.5).then_inc(lnST, 1)

            def dq_evac(m):
                for grp in range(2):
                    em.wge(sc, dqT, 2 * m + grp + 1)
                    em.wge(sc, dqW, 16 * (m - 1))
                    sc.activation(bufT[m % 2][:, grp * 1024:(grp + 1) * 1024],
                                  psDQ, AF.Identity).then_inc(dqE, 1)

            def ffn_scalar(phase):
                p = {"ffn1": 0, "ffn2": 1}[phase]
                for it in range(NTIL):
                    tile = p * NTIL + it
                    if phase == "ffn2" and it == 0:
                        em.wge(sc, cmb, 4 * NH)
                    ln_scalar(phase, it)
                    if tile > 0:
                        em.wge(sc, wBF, tile * 4 * HC)
                    if phase == "ffn2" and it == 0:
                        em.wge(sc, mmP, 128)
                    if tile == 0:
                        for m in range(21):
                            dq_evac(m)
                    for hc in range(HC):
                        ev = tile * HC + hc
                        em.wge(sc, mmA, ev + 1)
                        sc.activation(x2fm[:, hc * TT:(hc + 1) * TT], psA[hc % 2],
                                      AF.Relu, bias=bupc[:, hc:hc + 1]).then_inc(evA, 1)
                        if tile == 0:
                            dq_evac(hc + 21)
                    if tile == 0:
                        for m in range(HC + 21, 2 * HC):
                            dq_evac(m)
                    for dch in range(4):
                        for hc in range(HC):
                            wev = (tile * 4 + dch) * HC + hc
                            if tile == 0:
                                em.wge(sc, dqW,
                                       16 * (HC + (hc // 16) * 16 + dch * 4 + 4))
                            else:
                                em.wge(sc, dqW, 16 * 2 * HC)
                            em.wge(sc, wBF, wev - 3)
                            s0 = hc * 128
                            sc.dma_start(out=wbufB[hc % 4][:, :],
                                         in_=wdnT_c[s0:s0 + 128, dch * 512:(dch + 1) * 512]
                                         ).then_inc(wBL[hc % 4], 16)

            ffn_scalar("ffn1")
            if stage == "ffn1":
                return

            for it in range(NTIL):
                ln_scalar("attn", it)

            def cs_load_u(u, pc, tc):
                if u < 2:
                    em.wge(sc, lnN, 4 * 8)
                else:
                    em.wge(sc, evP, 2 * (u - 2) + 2)
                sl = u % 2
                sc.dma_start(out=cs2c[sl][:, :],
                             in_=cos_in[pc * 128:(pc + 1) * 128, tc * 512:(tc + 1) * 512]
                             ).then_inc(csL[sl], 16)
                sc.dma_start(out=cs2s[sl][:, :],
                             in_=sin_in[pc * 128:(pc + 1) * 128, tc * 512:(tc + 1) * 512]
                             ).then_inc(csL[sl], 16)

            for u in range(32):
                cs_load_u(u, u // 4, u % 4)
            for hg in range(4):
                for tsub in range(16):
                    vidx = hg * 16 + tsub
                    em.wge(sc, mmV, vidx + 1)
                    em.wge(sc, vstW[vidx % 2], 64 * (vidx // 2))
                    sc.activation(vst[vidx % 2][:, :], psV[vidx % 2], AF.Identity).then_inc(evV, 1)
            def elu_exp(gi):
                em.wge(sc, elV, gi + 1)
                sc.activation(tmpAB[:, :], tmpAB[:, :], AF.Exp).then_inc(elA, 1)

            for tc in range(4):
                cs_load_u(32 + tc, 0, tc)
            for g2 in range(4):
                elu_exp(g2)
            elu_exp(4)
            elu_exp(5)
            for h_idx in range(NH):
                if h_idx % 2 == 0 and h_idx // 2 + 1 <= 7:
                    p_ = h_idx // 2 + 1
                    for tc in range(2):
                        cs_load_u(32 + p_ * 4 + tc, p_, tc)
                for qc in range(4):
                    for kt in range(32):
                        si = sidx(h_idx, qc, kt)
                        em.wge(sc, mmS, si + 1)
                        if si >= 2:
                            em.wge(sc, pvK, si - 1)
                        sc.activation(ptb[kt % 2][:, 0:512], psS[kt % 2], AF.Exp,
                                      scale=INV_SQRT_HD, bias=cstc[:, 1:2]).then_inc(exS, 1)
                        if kt == 18 and h_idx < NH - 1:
                            elu_exp((h_idx + 1) * 6 + qc)
                        if qc == 3 and kt in (24, 28) and h_idx < NH - 1:
                            elu_exp((h_idx + 1) * 6 + 4 + (1 if kt == 28 else 0))
                        if kt == 20 and qc in (1, 2) and h_idx % 2 == 1 and h_idx < NH - 1:
                            p_ = (h_idx + 1) // 2
                            cs_load_u(32 + p_ * 4 + qc + 1, p_, qc + 1)

            if stage == "attn":
                return
            ffn_scalar("ffn2")

        @block.tensor
        def _(t):
            em.wge(t, sC, 128)
            em.wge(t, ini, 1)

            def dq_tr(n):
                em.wge(t, dqV, n + 1)
                for grp in range(2):
                    G = 2 * n + grp
                    em.wge(t, dqE, G)
                    for j in range(8):
                        dcc = grp * 8 + j
                        ins = t.transpose(psDQ3[:, j, :],
                                          bufD[n % 2][:, dcc * 128:(dcc + 1) * 128],
                                          ident[:, :])
                    ins.then_inc(dqT, 1)

            def ln_tensor(phase, it):
                lt = lt_of(phase, it)
                for tsub in range(4):
                    em.wge(t, lnN, 4 * lt + tsub + 1)
                    for grp in range(2):
                        gidx = (4 * lt + tsub) * 2 + grp
                        em.wge(t, lnE, gidx)
                        for j in range(8):
                            dcc = grp * 8 + j
                            ins = t.transpose(psLN3[:, j, :],
                                              lntm[tsub % 2][:, dcc * 128:(dcc + 1) * 128],
                                              ident[:, :])
                        ins.then_inc(lnT, 1)

            def ffn_tensor(phase):
                p = {"ffn1": 0, "ffn2": 1}[phase]
                for it in range(NTIL):
                    tile = p * NTIL + it
                    ln_tensor(phase, it)
                    if it > 0 or (phase == "ffn1" and tile > 0):
                        em.wge(t, evB, evb_ffn(tile - 1) + 16)
                    if phase == "ffn2" and it == 0:
                        em.wge(t, evB, 64 + 64)
                    em.wge(t, lnE, (4 * lt_of(phase, it) + 4) * 2)
                    for dcc in range(DC):
                        ins = t.matmul(psXA, loraA[:, dcc * R:(dcc + 1) * R],
                                       ln1fm[:, dcc * TT:(dcc + 1) * TT],
                                       start=(dcc == 0), stop=(dcc == DC - 1))
                    ins.then_inc(mmXA, 1)
                    em.wge(t, eXA, tile + 1)
                    if tile == 0:
                        for n in range(20):
                            dq_tr(n)
                    for hc in range(HC):
                        ev = tile * HC + hc
                        if tile == 0 and hc + 20 < HC + 20:
                            dq_tr(hc + 20)
                        em.wge(t, wAL[hc % 2], 16 * (ev // 2 + 1))
                        em.wge(t, evA, ev - 1)
                        for dcc in range(DC):
                            ins = t.matmul(psA[hc % 2], wbufA[hc % 2][:, dcc * 128:(dcc + 1) * 128],
                                           ln1fm[:, dcc * TT:(dcc + 1) * TT],
                                           start=(dcc == 0), stop=False)
                        ins.then_inc(wAF, 1)
                        ins = t.matmul(psA[hc % 2], loraB[:, hc * 128:(hc + 1) * 128], xa_sb[:, :],
                                       start=False, stop=True)
                        ins.then_inc(mmA, 1)
                        hx = hc - 2
                        if 0 <= hx < HC - 2:
                            em.wge(t, evA, tile * HC + hx + 1)
                            t.matmul(psXA, dnA[:, hx * R:(hx + 1) * R],
                                     x2fm[:, hx * TT:(hx + 1) * TT],
                                     start=(hx == 0), stop=False)
                    for hx in (HC - 2, HC - 1):
                        em.wge(t, evA, tile * HC + hx + 1)
                        ins = t.matmul(psXA, dnA[:, hx * R:(hx + 1) * R],
                                       x2fm[:, hx * TT:(hx + 1) * TT],
                                       start=False, stop=(hx == HC - 1))
                    ins.then_inc(mmXDA, 1)
                    if tile == 0:
                        for n in range(HC + 20, 2 * HC):
                            dq_tr(n)
                    em.wge(t, eXDA, tile + 1)
                    for dch in range(4):
                        if dch > 0:
                            em.wge(t, evB, evb_ffn(tile) + dch * 4)
                        elif it > 0 or (phase == "ffn1" and tile > 0):
                            em.wge(t, evB, evb_ffn(tile - 1) + 16)
                        elif phase == "ffn2":
                            em.wge(t, evB, 64 + 64)
                        for tsub in range(4):
                            t.matmul(psB[tsub], ones1[:, :], bdnr[:, dch * 512:(dch + 1) * 512],
                                     start=True, stop=False)
                            t.matmul(psB[tsub], xda_sb[:, tsub * 128:(tsub + 1) * 128],
                                     dnB[:, dch * 512:(dch + 1) * 512],
                                     start=False, stop=False)
                        for hc in range(HC):
                            wev = (tile * 4 + dch) * HC + hc
                            em.wge(t, wBL[hc % 4], 16 * (wev // 4 + 1))
                            for tsub in range(4):
                                ins = t.matmul(psB[tsub],
                                               x2fm[:, hc * TT + tsub * 128: hc * TT + (tsub + 1) * 128],
                                               wbufB[hc % 4][:, :],
                                               start=False, stop=(hc == HC - 1))
                            ins.then_inc(wBF, 1)

            ffn_tensor("ffn1")
            if stage == "ffn1":
                return

            em.wge(t, gsb, 1)
            t.matmul(psG, ones1[:, :], growb[0:1, :], start=True, stop=True).then_inc(mmG, 1)
            for it in range(NTIL):
                if it == 0:
                    em.wge(t, evG, 1)
                ln_tensor("attn", it)

            def proj_fill(u, kl, tc, which):
                em.wge(t, evP, 2 * u + which)
                em.wge(t, kwL[kl % 2], 64 * (kl // 2 + 1))
                for jc in range(DC):
                    ins = t.matmul(psPJ, wbufA[kl % 2][:, jc * 128:(jc + 1) * 128],
                                   ln2fm[:, jc * 2048 + tc * 512: jc * 2048 + tc * 512 + 512],
                                   start=(jc == 0), stop=(jc == DC - 1))
                ins.then_inc(mmP, 1)

            def free_pair_bufs(kl_lo, kl_hi, u_last):
                em.wge(t, evP, 2 * u_last + 2)
                ins = t.matmul(psPJ[:, 0:128], wbufA[kl_lo % 2][:, 0:128], ident[:, :],
                               start=True, stop=True)
                ins.then_inc(kwF, 1)
                ins = t.matmul(psPJ[:, 0:128], wbufA[kl_hi % 2][:, 0:128], ident[:, :],
                               start=True, stop=True)
                ins.then_inc(kwF, 1)

            for pc in range(8):
                kl_lo, kl_hi = pc * 2, pc * 2 + 1
                for tc in range(4):
                    u = pc * 4 + tc
                    if pc == 0:
                        em.wge(t, lnE, (4 * (4 + tc) + 4) * 2)
                    proj_fill(u, kl_lo, tc, 0)
                    proj_fill(u, kl_hi, tc, 1)
                free_pair_bufs(kl_lo, kl_hi, pc * 4 + 3)

            for hg in range(4):
                em.wge(t, vwL, 64 * (hg + 1))
                if hg == 0:
                    em.wge(t, evP, 2 * 32)
                for tsub in range(16):
                    vidx = hg * 16 + tsub
                    em.wge(t, evV, vidx - 1)
                    for jc in range(DC):
                        ins = t.matmul(psV[vidx % 2],
                                       ln2fm4[:, jc, tsub, :],
                                       wvbuf[:, jc * 512:(jc + 1) * 512],
                                       start=(jc == 0), stop=(jc == DC - 1))
                    ins.then_inc(mmV, 1)
                ins.then_inc(vwF, 1) if False else t.matmul(
                    psPJ[:, 0:128], wvbuf[:, 0:128], ident[:, :], start=True, stop=True
                ).then_inc(vwF, 1)

            for tc in range(4):
                proj_fill(32 + tc, 16, tc, 0)
                proj_fill(32 + tc, 17, tc, 1)
            free_pair_bufs(16, 17, 35)
            em.wge(t, kL[0], 32)
            em.wge(t, evP, 2 * 35 + 2)
            for g2 in range(4):
                em.wge(t, kfE, g2)
                for j in range(8):
                    st_ = g2 * 8 + j
                    ins = t.transpose(psDQ3[:, j, :],
                                      k_buf[0][:, st_ * 128:(st_ + 1) * 128],
                                      ident[:, :])
                ins.then_inc(kfT, 1)

            def mem_mm(h):
                mpar = h % 2
                em.wge(t, vL[mpar], 128 * (h // 2 + 1))
                em.wge(t, vOn, 1)
                em.wge(t, kfE, h * 4 + 4)
                if mpar == 0 and h > 0:
                    em.wge(t, evP, 2 * (32 + (h // 2) * 4 + 3) + 2)
                for kt in range(32):
                    ins = t.matmul(psMEM[:, 0:129], kf_tm[:, kt * 128:(kt + 1) * 128],
                                   v_buf[mpar][:, kt * 130:kt * 130 + 129],
                                   start=(kt == 0), stop=(kt == 31))
                ins.then_inc(mmM, 1)

            mem_mm(0)

            for h_idx in range(NH):
                hh = HORDER[h_idx]
                par = h_idx % 2
                pp = (h_idx // 2) % 2
                if h_idx == 0:
                    em.wge(t, evV, 64)
                if par == 0 and h_idx > 0:
                    em.wge(t, evP, 2 * (32 + (h_idx // 2) * 4 + 3) + 2)
                s_mm0 = None

                def s_mm(qc, kt):
                    si = sidx(h_idx, qc, kt)
                    if si >= 2:
                        em.wge(t, exS, si - 1)
                    ins = t.matmul(psS[kt % 2],
                                   k_buf[par][:, kt * 128:(kt + 1) * 128],
                                   qrp4[pp][par][:, qc * 512:(qc + 1) * 512],
                                   start=True, stop=True)
                    ins.then_inc(mmS, 1)

                s_mm(0, 0)
                s_mm(0, 1)
                em.wge(t, evM, h_idx + 1)
                em.wge(t, qfR, 2 * (h_idx + 1))
                for w in range(2):
                    if h_idx * 2 + w > 0:
                        em.wge(t, evMO, h_idx * 2 + w)
                    if w == 0 and h_idx > 0:
                        em.wge(t, cmb, h_idx * 4)
                    for j in range(8):
                        qsub = w * 8 + j
                        ins = t.matmul(psMO[j][:, 0:129], q_feat[:, qsub * 128:(qsub + 1) * 128],
                                       memSB[:, 0:129], start=True, stop=True)
                    ins.then_inc(mmMO, 1)
                def comb_flip(qc):
                    em.wge(t, cmb, h_idx * 4 + qc + 1)
                    em.wge(t, cmE, h_idx * 2 + qc // 2)
                    for j in range(4):
                        ins = t.transpose(psLN3[:, (qc % 2) * 4 + j, :],
                                          cmbTM[qc % 2][:, j * 128:(j + 1) * 128],
                                          ident[:, :])
                    ins.then_inc(cmf, 1)

                nh1 = h_idx + 1
                np_ = nh1 // 2

                for qc in range(4):
                    for kt in range(32):
                        si = sidx(h_idx, qc, kt)
                        if 0 < kt < 31:
                            s_mm(qc, kt + 1)
                        elif kt == 31 and qc < 3:
                            s_mm(qc + 1, 0)
                            s_mm(qc + 1, 1)
                        em.wge(t, exS, si + 1)
                        if kt == 0:
                            em.wge(t, evMO, h_idx * 2 + 2)
                            em.wge(t, cmb, h_idx * 4 + qc)
                        for j in range(4):
                            ins = t.matmul(psPV[j][:, 0:129],
                                           ptb[kt % 2][:, j * 128:(j + 1) * 128],
                                           v_buf[par][:, kt * 130:kt * 130 + 129],
                                           start=(kt == 0), stop=(kt == 31))
                        ins.then_inc(pvK, 1)
                        if kt == 2 and par == 1 and nh1 < NH:
                            em.wge(t, evM, h_idx + 1)
                            if qc == 0:
                                em.wge(t, qfR, max(0, 2 * (2 * np_ - 2)))
                            else:
                                em.wge(t, kfE, 4 * nh1 + qc)
                            proj_fill(32 + np_ * 4 + qc, 16 + 2 * np_, qc, 0)
                        if kt == 6 and par == 1 and nh1 < NH:
                            proj_fill(32 + np_ * 4 + qc, 17 + 2 * np_, qc, 1)
                        if kt == 16 and qc == 3 and par == 1 and nh1 < NH:
                            free_pair_bufs(16 + 2 * np_, 17 + 2 * np_, 32 + np_ * 4 + 3)
                        if nh1 < NH and ((kt == 14 and qc < 3) or (kt == 18 and qc == 3)):
                            g2 = qc
                            em.wge(t, kfE, 4 * nh1 + g2)
                            if g2 == 0:
                                em.wge(t, kL[nh1 % 2], 32 * (nh1 // 2 + 1))
                                em.wge(t, evM, h_idx + 1)
                            if par == 1:
                                em.wge(t, evP, 2 * (32 + np_ * 4 + g2) + 2)
                            for j in range(8):
                                st_ = g2 * 8 + j
                                ins = t.transpose(psDQ3[:, j, :],
                                                  k_buf[nh1 % 2][:, st_ * 128:(st_ + 1) * 128],
                                                  ident[:, :])
                            ins.then_inc(kfT, 1)
                    if qc >= 2:
                        comb_flip(qc - 2)
                if h_idx < NH - 1:
                    mem_mm(h_idx + 1)
                comb_flip(2)
                comb_flip(3)

            for it in range(NTIL):
                for ic in range(4):
                    if it * 4 + ic > 0:
                        em.wge(t, evB, 64 + (it * 4 + ic - 1) * 4 + 4)
                    else:
                        em.wge(t, pvK, 128 * NH)
                        em.wge(t, cmb, 4 * NH)
                        em.wge(t, evM, NH)
                    for jc in range(DC):
                        wev = (it * 4 + ic) * DC + jc
                        if ic == 0 and jc in (0, 8):
                            em.wge(t, woCL[jc // 8], 16 * 8 * (it + 1))
                        em.wge(t, woL[jc % 4], 16 * (wev // 4 + 1))
                        for tsub in range(4):
                            ins = t.matmul(psB[tsub],
                                           woCbuf[:, jc * 512 + tsub * 128: jc * 512 + (tsub + 1) * 128],
                                           wbufB[jc % 4][:, :],
                                           start=(jc == 0), stop=(jc == DC - 1))
                        ins.then_inc(woF, 1)
                        if ic == 3 and jc in (7, 15):
                            ins = t.matmul(psPJ[:, 0:128], woCbuf[:, jc * 512:jc * 512 + 128],
                                           ident[:, :], start=True, stop=True)
                            ins.then_inc(woCH, 1)
                ins = t.matmul(psPJ[:, 0:128], ident[:, :], ident[:, 0:128], start=True, stop=True)
                ins.then_inc(woCF, 1)

            if stage == "attn":
                return
            ffn_tensor("ffn2")

        @block.vector
        def _(v):
            vctr = {"n": 0}

            def vf(ins):
                vctr["n"] += 1
                ins.then_inc(vsy, 1)
                em.wge(v, vsy, vctr["n"])

            em.wge(v, sC, 128)
            v.memset(ones1[:, :], 1.0)
            v.memset(zrow[:, :], 0.0)
            v.memset(zb512[:, :], 0.0)
            v.memset(cstc[:, 0:1], EPS)
            v.memset(cstc[:, 1:2], EXP_BIAS).then_inc(ini, 1)

            def dq_vector():
                for n in range(2 * HC):
                    par = n % 2
                    em.wge(v, dqL[par], 80 * (n // 2 + 1))
                    em.wge(v, dqT, 2 * n - 2)
                    sview = bufS[par][:, :].unsqueeze(2).broadcast_to([128, 32, 64])
                    qview = bufQ[par][:, :].rearrange("p (b i) -> p b i", i=64)
                    dview = bufD[par][:, :].rearrange("p (b i) -> p b i", i=64)
                    v.scalar_tensor_tensor(dview, qview, -7.5, sview, ALU.add, ALU.mult).then_inc(dqV, 1)

            def ln_vector(phase, it, dst_fn, mm_gate=None):
                lt = lt_of(phase, it)
                xph = {"ffn1": "ffn1", "ffn2": "ffn2", "attn": "ln2"}[phase]
                for tsub in range(4):
                    em.wge(v, sXt[tsub], 16 * ((xev(xph, it, 0) // 4) + 1))
                    v.reduce_sum(stat[:, tsub:tsub + 1], xt[tsub][:, :], axis=AX.X).then_inc(lnR, 1)
                    em.wge(v, lnQ, 4 * lt + tsub + 1)
                    v.reduce_sum(stat[:, 4 + tsub:5 + tsub], sq[:, :], axis=AX.X).then_inc(lnR, 1)
                em.wge(v, lnR, 8 * lt + 8)
                vf(v.tensor_scalar(stat[:, 0:4], stat[:, 0:4], INV_D, None, ALU.mult))
                vf(v.tensor_scalar(stat[:, 4:8], stat[:, 4:8], INV_D, None, ALU.mult))
                vf(v.tensor_mul(stat[:, 8:12], stat[:, 0:4], stat[:, 0:4]))
                v.tensor_sub(stat[:, 4:8], stat[:, 4:8], stat[:, 8:12]).then_inc(lnV, 1)
                em.wge(v, lnST, lt + 1)
                if mm_gate is not None:
                    mm_gate()

                def emit_evacs(ts):
                    for grp in range(2):
                        gidx = (4 * lt + ts) * 2 + grp
                        em.wge(v, lnT, gidx + 1)
                        v.tensor_copy(dst_fn(grp, ts), psLN3[:, 0:8, :]).then_inc(lnE, 1)

                for tsub in range(4):
                    if tsub >= 2:
                        em.wge(v, lnT, (4 * lt + tsub - 1) * 2)
                    v.tensor_scalar(lntm[tsub % 2][:, :], xt[tsub][:, :],
                                    stat[:, tsub:tsub + 1], stat[:, 12 + tsub:13 + tsub],
                                    ALU.subtract, ALU.mult).then_inc(lnN, 1)
                    if tsub >= 1:
                        emit_evacs(tsub - 1)
                em.wge(v, lnN, 4 * lt + 4)
                emit_evacs(3)

            def ffn_vector(phase):
                p = {"ffn1": 0, "ffn2": 1}[phase]
                for it in range(NTIL):
                    tile = p * NTIL + it

                    def gate():
                        if tile > 0:
                            em.wge(v, mmA, tile * HC)
                            em.wge(v, mmXA, tile)
                        if phase == "ffn2" and it == 0:
                            em.wge(v, mmP, 128)
                            em.wge(v, mmS, 128 * NH)

                    def dst_fn(grp, tsub):
                        return ln1fm4[:, grp * 8:(grp + 1) * 8, tsub, :]

                    if phase == "ffn2" and it == 0:
                        em.wge(v, woCF, NTIL)
                        em.wge(v, qfR, 2 * NH)
                    ln_vector(phase, it, dst_fn, mm_gate=gate)
                    em.wge(v, mmXA, tile + 1)
                    v.tensor_copy(xa_sb[:, :], psXA).then_inc(eXA, 1)
                    if tile == 0:
                        dq_vector()
                    em.wge(v, mmXDA, tile + 1)
                    v.tensor_copy(xda_sb[:, :], psXA).then_inc(eXDA, 1)
                    for dch in range(4):
                        em.wge(v, wBF, (tile * 4 + dch + 1) * HC)
                        for tsub in range(4):
                            em.wge(v, oS[tsub % 2],
                                   16 * (OS_B[phase] + (it * 4 + dch) * 2 + tsub // 2))
                            v.scalar_tensor_tensor(outst[tsub % 2][:, :], psB[tsub], 0.5,
                                                   xt[tsub][:, dch * 512:(dch + 1) * 512],
                                                   ALU.mult, ALU.add).then_inc(evB, 1)

            ffn_vector("ffn1")
            if stage == "ffn1":
                return

            em.wge(v, ssy, 1)
            v.tensor_copy(growb[0:1, :], grow[0:1, :]).then_inc(gsb, 1)
            em.wge(v, mmG, 1)
            ins = v.tensor_copy(gcol[:, :], psG)
            ins.then_inc(evG, 1)
            em.wge(v, evG, 1)
            vf(v.tensor_scalar(gicol[:, :], gcol[:, :], -1.0, 1.0, ALU.mult, ALU.add))

            for it in range(NTIL):
                def gate2():
                    em.wge(v, wBF, 4 * NTIL * HC)

                def dst2(grp, tsub, _it=it):
                    return ln2fm4[:, grp * 8:(grp + 1) * 8, _it * 4 + tsub, :]

                ln_vector("attn", it, dst2, mm_gate=gate2)

            em.wge(v, dqW, 16 * 2 * HC)
            for par in range(2):
                ones_view = v_buf[par].ap().rearrange("p (kt c) -> p kt c", c=130)[:, :, 128:129]
                ins = v.memset(ones_view, 1.0)
            ins.then_inc(vOn, 1)

            for pc in range(8):
                for tc in range(4):
                    u = pc * 4 + tc
                    em.wge(v, csL[u % 2], 32 * (u // 2 + 1))
                    em.wge(v, mmP, 2 * u + 1)
                    ins = v.tensor_copy(klo_sb[:, :], psPJ)
                    ins.then_inc(evP, 1)
                    em.wge(v, evP, 2 * u + 1)
                    em.wge(v, mmP, 2 * u + 2)
                    csl = cs2c[u % 2][:, :]
                    ssl = cs2s[u % 2][:, :]
                    em.wge(v, kwW[0], 16 * u)
                    em.wge(v, kwW[1], 16 * u)
                    vf(v.tensor_mul(tmpAB[:, 0:512], klo_sb[:, :], csl))
                    vf(v.tensor_mul(tmpAB[:, 512:1024], psPJ, ssl))
                    vf(v.tensor_sub(kst[0][:, :], tmpAB[:, 0:512], tmpAB[:, 512:1024]))
                    vf(v.tensor_mul(tmpAB[:, 0:512], psPJ, csl))
                    vf(v.tensor_mul(tmpAB[:, 512:1024], klo_sb[:, :], ssl))
                    ins = v.tensor_add(kst[1][:, :], tmpAB[:, 0:512], tmpAB[:, 512:1024])
                    ins.then_inc(evP, 1)
                    em.wge(v, evP, 2 * u + 2)
                v.engine_nop().then_inc(csF, 1)

            def q_rope_tc(p, tc, chain):
                em.wge(v, elA, chain)
                u = 32 + p * 4 + tc
                em.wge(v, csL[u % 2], 32 * (u // 2 + 1))
                em.wge(v, mmP, 2 * u + 1)
                ins = v.tensor_copy(klo_sb[:, :], psPJ)
                ins.then_inc(evP, 1)
                em.wge(v, evP, 2 * u + 1)
                em.wge(v, mmP, 2 * u + 2)
                csl = cs2c[u % 2][:, :]
                ssl = cs2s[u % 2][:, :]
                vf(v.tensor_mul(tmpAB[:, 0:512], klo_sb[:, :], csl))
                vf(v.tensor_mul(tmpAB[:, 512:1024], psPJ, ssl))
                vf(v.tensor_sub(qrp4[p % 2][0][:, tc * 512:(tc + 1) * 512],
                                tmpAB[:, 0:512], tmpAB[:, 512:1024]))
                vf(v.tensor_mul(tmpAB[:, 0:512], psPJ, csl))
                vf(v.tensor_mul(tmpAB[:, 512:1024], klo_sb[:, :], ssl))
                ins = v.tensor_add(qrp4[p % 2][1][:, tc * 512:(tc + 1) * 512],
                                   tmpAB[:, 0:512], tmpAB[:, 512:1024])
                ins.then_inc(evP, 1)
                em.wge(v, evP, 2 * u + 2)
                if tc == 3:
                    v.engine_nop().then_inc(csF, 1)

            def kelu(h, g):
                gi = h * 6 + g
                em.wge(v, kfT, h * 4 + g + 1)
                if gi > 0:
                    em.wge(v, elA, gi)
                ins = v.tensor_scalar(tmpAB[:, :], psDQ, 0.0, None, ALU.min)
                ins.then_inc(elV, 1)
                em.wge(v, elA, gi + 1)
                ins = v.scalar_tensor_tensor(kf_tm[:, g * 1024:(g + 1) * 1024],
                                             psDQ, 0.0, tmpAB[:, :],
                                             ALU.max, ALU.add)
                ins.then_inc(kfE, 1)
                em.wge(v, kfE, h * 4 + g + 1)

            def q_elu(h):
                hpar = h % 2
                hpp = (h // 2) % 2
                em.wge(v, evP, 2 * (32 + (h // 2) * 4 + 3) + 2)
                for hf in range(2):
                    gi = h * 6 + 4 + hf
                    em.wge(v, elA, gi)
                    ins = v.tensor_scalar(tmpAB[:, :], qrp4[hpp][hpar][:, hf * 1024:(hf + 1) * 1024],
                                          0.0, None, ALU.min)
                    ins.then_inc(elV, 1)
                    em.wge(v, elA, gi + 1)
                    ins = v.scalar_tensor_tensor(q_feat[:, hf * 1024:(hf + 1) * 1024],
                                                 qrp4[hpp][hpar][:, hf * 1024:(hf + 1) * 1024],
                                                 0.0, tmpAB[:, :], ALU.max, ALU.add)
                    ins.then_inc(qfR, 1)
                    em.wge(v, qfR, 2 * h + hf + 1)

            em.wge(v, mmV, 64)
            for tc in range(4):
                q_rope_tc(0, tc, 0)
            for g2 in range(4):
                kelu(0, g2)
            q_elu(0)

            for h_idx in range(NH):
                hh = HORDER[h_idx]
                par = h_idx % 2
                pp = (h_idx // 2) % 2
                em.wge(v, mmM, h_idx + 1)
                ins = v.tensor_copy(memSB[:, 0:130], psMEM)
                ins.then_inc(evM, 1)
                em.wge(v, evM, h_idx + 1)
                for w in range(2):
                    em.wge(v, mmMO, h_idx * 2 + w + 1)
                    for j in range(8):
                        qsub = w * 8 + j
                        ins = v.tensor_copy(memout[:, qsub * 130:(qsub + 1) * 130],
                                            psMO[j][:, 0:130])
                    ins.then_inc(evMO, 1)
                    em.wge(v, evMO, h_idx * 2 + w + 1)
                denl = memout.ap().rearrange("p (q c) -> p q c", c=130)[:, :, 128]
                vf(v.reciprocal(stat[:, 0:16], denl))
                vf(v.tensor_scalar(stat[:, 0:16], stat[:, 0:16], gcol[:, hh:hh + 1], None, ALU.mult))
                for qsub in range(16):
                    vf(v.tensor_scalar(tmpP16[:, qsub * 128:(qsub + 1) * 128],
                                       memout[:, qsub * 130:qsub * 130 + 128],
                                       stat[:, qsub:qsub + 1], None, ALU.mult))
                for qc in range(4):
                    if h_idx % 2 == 1 and h_idx < NH - 1:
                        q_rope_tc((h_idx + 1) // 2, qc, h_idx * 6 + 6 + qc)
                    if h_idx < NH - 1:
                        kelu(h_idx + 1, qc)
                        if qc == 3:
                            q_elu(h_idx + 1)
                    em.wge(v, pvK, (h_idx * 4 + qc) * 32 + 32)
                    if h_idx * 4 + qc >= 2:
                        em.wge(v, cmf, h_idx * 4 + qc - 1)
                    for j in range(4):
                        ins = v.tensor_copy(den_sb[:, j:j + 1], psPV[j][:, 128:129])
                    vf(ins)
                    vf(v.reciprocal(den_sb[:, 0:4], den_sb[:, 0:4]))
                    vf(v.tensor_scalar(den_sb[:, 0:4], den_sb[:, 0:4], gicol[:, hh:hh + 1], None, ALU.mult))
                    for j in range(4):
                        qsub = qc * 4 + j
                        vf(v.scalar_tensor_tensor(cmbTM[qc % 2][:, j * 128:(j + 1) * 128],
                                                  psPV[j][:, 0:128], den_sb[:, j:j + 1],
                                                  tmpP16[:, qsub * 128:(qsub + 1) * 128],
                                                  ALU.mult, ALU.add))
                    v.engine_nop().then_inc(cmb, 1)
                for g2 in range(2):
                    em.wge(v, cmf, h_idx * 4 + (g2 + 1) * 2)
                    if h_idx > 0:
                        em.wge(v, cmW[g2], 16 * h_idx)
                    ins = v.tensor_copy(cmbOut[g2][:, :], psLN[:, :])
                    ins.then_inc(cmE, 1)
                    em.wge(v, cmE, h_idx * 2 + g2 + 1)

            for it in range(NTIL):
                for ic in range(4):
                    em.wge(v, woF, (it * 4 + ic + 1) * DC)
                    for tsub in range(4):
                        if ic == 0:
                            em.wge(v, sXt[tsub], 16 * ((xev("wo", it, 0) // 4) + 1))
                        em.wge(v, oS[tsub % 2],
                               16 * (OS_B["wo"] + (it * 4 + ic) * 2 + tsub // 2))
                        v.tensor_tensor(outst[tsub % 2][:, :], psB[tsub],
                                        xt[tsub][:, ic * 512:(ic + 1) * 512],
                                        ALU.add).then_inc(evB, 1)

            if stage == "attn":
                return
            ffn_vector("ffn2")

    return nc


def _prep_core_inputs(inputs, core):
    b, half = core // 2, core % 2
    x = np.asarray(inputs["L0_attn_hidden_states"])
    xs = np.ascontiguousarray(x[b, half * T:(half + 1) * T, :], dtype=np.float32)

    s0 = half * T
    t = np.arange(s0, s0 + T, dtype=np.float32)
    inv_freq = (1.0 / (10000.0 ** (np.arange(0, D, 2, dtype=np.float32) / np.float32(D)))).astype(np.float32)
    freqs = (inv_freq[:, None] * t[None, :]).astype(np.float32)
    cos = np.cos(freqs).astype(np.float32)
    sin = np.sin(freqs).astype(np.float32)

    bup = np.ascontiguousarray(np.asarray(inputs["L0_ffn_b_up"], np.float32).reshape(HC, 128).T)
    bdn = np.ascontiguousarray(np.asarray(inputs["L0_ffn_b_down"], np.float32).reshape(1, D))
    gate = np.ascontiguousarray(np.asarray(inputs["L0_attn_gate"], np.float32).reshape(1, NH))

    return dict(
        x_in=xs,
        wq_in=np.asarray(inputs["L0_attn_Wq"], np.float32),
        wk_in=np.asarray(inputs["L0_attn_Wk"], np.float32),
        wv_in=np.asarray(inputs["L0_attn_Wv"], np.float32),
        wo_in=np.asarray(inputs["L0_attn_Wo"], np.float32),
        gate_in=gate,
        wupq_in=np.asarray(inputs["L0_ffn_w_up_q"], np.int32),
        wups_in=np.asarray(inputs["L0_ffn_w_up_scale"], np.float32),
        bup_in=bup,
        ua_in=np.ascontiguousarray(np.asarray(inputs["L0_ffn_w_up_lora_a"], np.float32)
                                   .reshape(DC, 128, R).transpose(1, 0, 2).reshape(128, DC * R)),
        ub_in=np.asarray(inputs["L0_ffn_w_up_lora_b"], np.float32),
        wdnq_in=np.asarray(inputs["L0_ffn_w_down_q"], np.int32),
        wdns_in=np.asarray(inputs["L0_ffn_w_down_scale"], np.float32),
        bdn_in=bdn,
        da_in=np.ascontiguousarray(np.asarray(inputs["L0_ffn_w_down_lora_a"], np.float32)
                                   .reshape(HC, 128, R).transpose(1, 0, 2).reshape(128, HC * R)),
        db_in=np.asarray(inputs["L0_ffn_w_down_lora_b"], np.float32),
        cos_in=cos,
        sin_in=sin,
        ident_in=np.eye(128, dtype=ml_dtypes.bfloat16),
    )


def kernel(**inputs):
    in_maps = [_prep_core_inputs(inputs, c) for c in range(8)]
    nc = build(_STAGE)
    res = run_bass_kernel_spmd(nc, in_maps, core_ids=list(range(8)))
    out = np.zeros((4, SEQ, D), dtype=np.float32)
    for c in range(8):
        b, half = c // 2, c % 2
        out[b, half * T:(half + 1) * T, :] = np.asarray(res.results[c]["out_d"])
    return out
